# version 09
# baseline (speedup 1.0000x reference)
import math
import numpy as np
import ml_dtypes

import concourse.bass as bass
import concourse.mybir as mybir
from concourse.bass_utils import run_bass_kernel_spmd

F32 = mybir.dt.float32
BF16 = mybir.dt.bfloat16
AF = mybir.ActivationFunctionType
ALU = mybir.AluOpType
BF = ml_dtypes.bfloat16

H, R, D, F, FF = 16, 16, 64, 1024, 4096
C, B, NG = 128, 16, 2
LMIN = -5.0
TLAM_CAP = 40.0
EPS = 1e-5
RND = float(2 ** 23)
SCL = 0.125
STUB = set()


def pitch(h):
    return h[:].ap[0][0]


class TV:

    def __init__(self, h, base, ncols):
        self.h, self.base, self.ncols = h, base, ncols
        self.pitch = pitch(h)

    def ap(self, part_off, free_off, dims):
        full = self.h[:]
        return bass.AP(full.tensor,
                       full.offset + part_off * self.pitch + self.base + free_off,
                       [[self.pitch if s == "P" else s, n] for s, n in dims])

    def sl(self, a, b, p0=0, p1=128):
        return self.ap(p0, a, [["P", p1 - p0], [1, b - a]])


def rawap(h, part_off, free_off, dims):
    if isinstance(h, TV):
        return h.ap(part_off, free_off, dims)
    full = h[:]
    p = full.ap[0][0]
    return bass.AP(full.tensor, full.offset + part_off * p + free_off,
                   [[p if s == "P" else s, n] for s, n in dims])


def dslice(h, a, b, p0=0, p1=128):
    if isinstance(h, TV):
        return h.sl(a, b, p0, p1)
    return h[p0:p1, a:b]



NSLOT = 8


class Op:
    __slots__ = ("eng", "fn", "deps", "dma", "coll", "need_inc", "sem", "sem_val",
                 "tag", "emitted")

    def __init__(self, eng, fn, deps, dma, tag="", coll=False):
        self.eng, self.fn, self.dma, self.tag = eng, fn, dma, tag
        self.coll = coll
        self.deps = [d for d in deps if d is not None]
        self.need_inc = False
        self.sem = None
        self.sem_val = None
        self.emitted = False


class Prog:

    ENGS = ("T", "V", "S", "G", "Q")
    DMAE = ("S", "G", "Q")

    def __init__(self, nc, stack):
        self.nc = nc
        self.ops = {e: [] for e in self.ENGS}
        self.sems = {e: stack.enter_context(nc.semaphore(f"sem_{e}")) for e in self.ENGS}
        self.dsem = {e: [stack.enter_context(nc.semaphore(f"dma_{e}{i}"))
                         for i in range(NSLOT)] for e in self.DMAE}
        self.csem = stack.enter_context(nc.semaphore("sem_cc"))
        self.tot = {e: 0 for e in self.ENGS}
        self.dcnt = {e: 0 for e in self.DMAE}
        self.ccnt = 0

    def add(self, eng, fn, deps=(), dma=False, tag="", coll=False):
        op = Op(eng, fn, deps, dma, tag, coll)
        self.ops[eng].append(op)
        return op

    def emit_block(self):
        nc = self.nc
        for e in self.ENGS:
            for op in self.ops[e]:
                if op.dma or op.coll:
                    op.need_inc = True
                for d in op.deps:
                    if d.emitted:
                        continue
                    if d.eng == op.eng == "T" and not (d.dma or d.coll):
                        continue
                    d.need_inc = True
        for e in self.ENGS:
            t = self.tot[e]
            for op in self.ops[e]:
                if op.coll:
                    self.ccnt += 1
                    op.sem = self.csem
                    op.sem_val = self.ccnt
                elif op.dma:
                    j = self.dcnt[e]
                    self.dcnt[e] = j + 1
                    op.sem = self.dsem[e][j % NSLOT]
                    op.sem_val = 16 * (j // NSLOT + 1)
                elif op.need_inc:
                    t += 1
                    op.sem = self.sems[e]
                    op.sem_val = t
            self.tot[e] = t
        engmap = dict(T="tensor", V="vector", S="scalar", G="gpsimd", Q="sync")
        with nc.Block() as blk:
            for e in self.ENGS:
                if not self.ops[e]:
                    continue
                ops_e = self.ops[e]

                def body(engine, ops_e=ops_e, e=e):
                    wm = {}
                    for op in ops_e:
                        if op.dma and op.sem_val > 16:
                            key = id(op.sem)
                            pv = op.sem_val - 16
                            if wm.get(key, 0) < pv:
                                engine.wait_ge(op.sem, pv)
                                wm[key] = pv
                        for d in op.deps:
                            if d.emitted and not (d.dma or d.coll):
                                continue
                            if d.eng == op.eng == "T" and not (d.dma or d.coll):
                                continue
                            key = id(d.sem)
                            if wm.get(key, 0) >= d.sem_val:
                                continue
                            engine.wait_ge(d.sem, d.sem_val)
                            wm[key] = d.sem_val
                        inst = op.fn(engine)
                        if op.need_inc:
                            assert inst is not None, op.tag
                            inst.then_inc(op.sem, 16 if op.dma else 1)
                getattr(blk, engmap[e])(body)
        for e in self.ENGS:
            for op in self.ops[e]:
                op.emitted = True
        self.ops = {e: [] for e in self.ENGS}



def host_tables():
    t = {}
    e = np.zeros((1, 8 * C), np.float32)
    for j in range(8):
        for c in range(C):
            e[0, j * C + c] = c - B * j - 8
    t["e_tlam"] = e
    t["e_tplus"] = (np.arange(C, dtype=np.float32) + 1.0)[None, :]
    t["e_tminus"] = (127.0 - np.arange(C, dtype=np.float32))[None, :]
    t["e_tp"] = (8.0 - np.arange(B, dtype=np.float32))[None, :]
    m = np.zeros((128, 8 * C), BF)
    for hl in range(8):
        for sp in range(B):
            for j in range(8):
                for c in range(C):
                    if c >= B * j + sp:
                        m[hl * B + sp, j * C + c] = 1
    t["tri_mask"] = m
    bm = np.zeros((128, 128), BF)
    for p in range(4):
        for e_ in range(2):
            bm[32 * p + 16 * e_:32 * p + 16 * e_ + 16, 64 * e_:64 * e_ + 64] = 1
    t["blk_mask"] = bm
    t["ones_row_b"] = np.ones((1, 512), BF)
    return t


def host_weights(w):
    out = {}
    perm = np.zeros(F, np.int64)
    for par in range(2):
        for h_ in range(H):
            for i in range(D // 2):
                perm[par * 512 + h_ * 32 + i] = h_ * D + 2 * i + par
    out["qT"] = np.ascontiguousarray(w["L0_attn_q_w"][perm].T)
    out["kT"] = np.ascontiguousarray(w["L0_attn_k_w"][perm].T)

    def bd(Wq):
        m = np.zeros((F, H * R), np.float32)
        par_i = np.arange(D) % 2
        i_i = np.arange(D) // 2
        for h_ in range(H):
            rows = par_i * 512 + h_ * 32 + i_i
            m[rows, R * h_:R * h_ + R] = Wq[h_].T
        return m
    out["wqbd"] = bd(w["L0_attn_Wq"])
    out["wkbd"] = bd(w["L0_attn_Wk"])
    out["g_wT"] = np.ascontiguousarray(w["L0_attn_g_w"].T)
    out["g_b_row"] = np.ascontiguousarray(w["L0_attn_g_b"][None, :])
    out["v_wT"] = np.ascontiguousarray(w["L0_attn_v_w"].T)
    out["out_wT"] = np.ascontiguousarray(w["L0_attn_out_w"].T)
    out["out_b_row"] = np.ascontiguousarray(w["L0_attn_out_b"][None, :])
    out["ln_g"] = np.ascontiguousarray(w["L0_attn_ln_g"][None, :])
    out["ln_b"] = np.ascontiguousarray(w["L0_attn_ln_b"][None, :])
    out["wfcT"] = np.ascontiguousarray(w["L0_ffn_W_fc"].T)
    out["b_fc_row"] = np.ascontiguousarray(w["L0_ffn_b_fc"][None, :])
    out["a_fcT"] = np.ascontiguousarray(w["L0_ffn_A_fc"].T)
    out["b_fcBT"] = np.ascontiguousarray(w["L0_ffn_B_fc"].T)
    out["wprojT"] = np.ascontiguousarray(w["L0_ffn_W_proj"].T)
    out["b_proj_row"] = np.ascontiguousarray(w["L0_ffn_b_proj"][None, :])
    out["a_projT"] = np.ascontiguousarray(w["L0_ffn_A_proj"].T)
    out["b_projBT"] = np.ascontiguousarray(w["L0_ffn_B_proj"].T)
    out["decay_logits"] = np.ascontiguousarray(
        w["L0_attn_base_decay_logits"].reshape(1, -1))
    return out


_ROPE_CACHE = {}


def host_rope(Tfull, t0, T):
    key = (Tfull, t0, T)
    if key in _ROPE_CACHE:
        return _ROPE_CACHE[key]
    import jax
    import jax.numpy as jnp
    with jax.default_device(jax.devices("cpu")[0]):
        half = D // 2
        inv_freq = 1.0 / 10000.0 ** (jnp.arange(half, dtype=jnp.float32) / half)
        freqs = jnp.arange(Tfull, dtype=jnp.float32)[:, None] * inv_freq[None, :]
        cos = np.asarray(jnp.cos(freqs), np.float32)[t0:t0 + T]
        sin = np.asarray(jnp.sin(freqs), np.float32)[t0:t0 + T]
    cosT = np.ascontiguousarray(np.tile(cos.T, (4, 1))).astype(BF)
    sinT = np.ascontiguousarray(np.tile(sin.T, (4, 1))).astype(BF)
    _ROPE_CACHE[key] = (cosT, sinT)
    return cosT, sinT



def build_nc(T, n_ranks=8, max_phase=9, dbg_spills=False):
    NC = T // C
    NM = T // 512
    NTT = T // 128
    NM5 = T // 256
    nc = bass.Bass("TRN2", target_bir_lowering=False)

    def din(name, shape, dt=F32):
        return nc.dram_tensor(name, list(shape), dt, kind="ExternalInput").ap()

    x_in = din("x", [T, F])
    qT_d = din("qT", [F, F]); kT_d = din("kT", [F, F])
    wqbd_d = din("wqbd", [F, 256]); wkbd_d = din("wkbd", [F, 256])
    gwT_d = din("g_wT", [F, 256]); gb_d = din("g_b_row", [1, 256])
    vwT_d = din("v_wT", [F, F])
    owT_d = din("out_wT", [F, F]); ob_d = din("out_b_row", [1, F])
    lng_d = din("ln_g", [1, F]); lnb_d = din("ln_b", [1, F])
    wfcT_d = din("wfcT", [F, FF]); bfc_d = din("b_fc_row", [1, FF])
    afcT_d = din("a_fcT", [F, 16]); bfcBT_d = din("b_fcBT", [16, FF])
    wprT_d = din("wprojT", [FF, F]); bpr_d = din("b_proj_row", [1, F])
    aprT_d = din("a_projT", [FF, 16]); bprBT_d = din("b_projBT", [16, F])
    logit_d = din("decay_logits", [1, 256])
    cos_d = din("cosT", [128, T], BF16); sin_d = din("sinT", [128, T], BF16)
    etlam_d = din("e_tlam", [1, 8 * C]); etp_d = din("e_tplus", [1, C])
    etm_d = din("e_tminus", [1, C]); etpp_d = din("e_tp", [1, B])
    k128_d = din("k128row", [1, NC])
    trim_d = din("tri_mask", [128, 8 * C], BF16)
    blkm_d = din("blk_mask", [128, 128], BF16)
    onesb_d = din("ones_row_b", [1, 512], BF16)
    msel_d = din("msel", [1, 8])
    gap_d = din("gap4096", [1, 8])

    out_d = nc.dram_tensor("out", [T, F], F32, kind="ExternalOutput").ap()

    _k = dict(kind="ExternalOutput") if dbg_spills else {}
    if dbg_spills:
        ln2f_d = nc.dram_tensor("ln2f_dbg", [128, (T // 256) * 2048], BF16, **_k).ap()
        h1_d = nc.dram_tensor("h1_dbg", [128, (T // 256) * 8192], BF16, **_k).ap()
    else:
        ln2f_d = h1_d = None
    yfm_d = nc.dram_tensor("yfm_spill", [128, T * 8], BF16, **_k).ap()
    xnew_d = nc.dram_tensor("xnew_spill", [T, F], F32, **_k).ap()
    v_d = nc.dram_tensor("v_spill", [T, F], BF16, **_k).ap()
    at_d = nc.dram_tensor("at_spill", [128, NG * T], BF16, **_k).ap()
    qr_d = nc.dram_tensor("qr_spill", [128, NG * T], BF16, **_k).ap()
    wqf_d = nc.dram_tensor("wqf_spill", [128, 8 * FF], BF16, **_k).ap()
    wqp_d = nc.dram_tensor("wqp_spill", [128, 32 * F], BF16, **_k).ap()
    cc_in = nc.dram_tensor("cc_in", [NG * 128, 128], F32)
    cc_out = nc.dram_tensor("cc_out", [n_ranks * NG * 128, 128], F32)
    srow_d = nc.dram_tensor("srow_d", [1, 8], F32)

    import contextlib
    with contextlib.ExitStack() as stack:
        def sbuf(name, p, f_, dt):
            return stack.enter_context(nc.sbuf_tensor(name, [p, f_], dt))

        P = Prog(nc, stack)

        _last_tr = [None]

        def add_tr(fn, deps=(), tag="", eng="Q"):
            op = P.add(eng, fn, deps=list(deps) + ([_last_tr[0]] if _last_tr[0] else []),
                       dma=True, tag=tag)
            _last_tr[0] = op
            return op

        Pst = sbuf("Pst", 128, max(NG * (NC + 1) * 128, 8192), F32)
        stg = sbuf("stg", 128, 4096, F32)
        x_t = sbuf("x_t", 128, 2 * F, F32)
        mu_s = sbuf("mu_s", 128, NTT, F32)
        rs_s = sbuf("rs_s", 128, NTT, F32)
        mu2_s = sbuf("mu2_s", 128, NTT, F32)
        rs2_s = sbuf("rs2_s", 128, NTT, F32)
        st4 = sbuf("st4", 128, 32, F32)
        scr = sbuf("scr", 128, 256, F32)
        lam_t = sbuf("lam_t", 128, NG, F32)
        L_t = sbuf("L_t", 128, NG, F32)
        Lc_t = sbuf("Lc_t", 128, NG, F32)
        lamC_t = sbuf("lamC_t", 128, NG, F32)
        tk_t = sbuf("tk_t", 128, NG * NC, F32)
        wj_t = sbuf("wj_t", 128, NG * 8, F32)
        Msta = sbuf("Msta", 128, NG * 256, F32)
        rows_b = sbuf("rows_b", 1, 4096, BF16)
        ones_b = sbuf("ones_b", 1, 512, BF16)
        lntm = sbuf("lntm", 128, 2 * F, BF16)
        yfmc = sbuf("yfmc", 128, 2 * 8 * 128, BF16)
        tlam = sbuf("tlam", 128, NG * 8 * C, BF16)
        tplus = sbuf("tplus", 128, NG * C, BF16)
        tminfm = sbuf("tminfm", 128, NG * C, BF16)
        tp_t = sbuf("tp_t", 128, NG * B, BF16)
        tri_m = sbuf("tri_m", 128, 8 * C, BF16)
        blk_m = sbuf("blk_m", 128, 128, BF16)
        lafc = sbuf("lafc", 128, 8 * 16, BF16)
        lapr = sbuf("lapr", 128, 32 * 16, BF16)
        lbpr = sbuf("lbpr", 16, F, BF16)
        lora_b = sbuf("lora_b", 16, 1024, BF16)
        lbfc = sbuf("lbfc", 16, FF, BF16)
        lbfcr = sbuf("lbfcr", 1, FF, BF16)
        ki32 = sbuf("ki32", 128, 1024, mybir.dt.int32)

        ARN = sbuf("ARN", 128, 48128, BF16)

        def tv(base, ncols):
            return TV(ARN, base, ncols)

        w_vT = tv(0, 8192)
        w_gT = tv(8192, 2048)
        w_qbd = tv(10240, 2048)
        w_kbd = tv(12288, 2048)
        wqk_s = tv(14336, 4096)
        ln0f = tv(18432, 4096)
        qrot = tv(22528, 4096)
        krot = tv(26624, 4096)
        rtmp = tv(30720, 2048)
        g_fm = tv(32768, 1024)
        kr_fm = tv(33792, 1024)
        a_fm = tv(34816, 1024)
        amf = tv(35840, 1024)
        vck = tv(36864, 4096)
        at_mb = tv(40960, 2048)
        am_mb = tv(43008, 2048)
        qr_mb = tv(45056, 2048)
        cs_s = tv(47104, 1024)
        qt_b = tv(0, 2048)
        ssb = tv(2048, 2048)
        qp_b = tv(4096, 256)
        mkb = tv(4352, 256)
        qrc = tv(4608, 512)
        abd = tv(5120, 4096)
        vbd = tv(9216, 16384)
        mkbd = tv(25600, 8192)
        w_oT = tv(0, 8192)
        lng_r = TV(Pst, 0, 1024)
        lnb_r = TV(Pst, 1024, 1024)
        fcW = tv(0, 8192)
        pjW = tv(8192, 8192)
        ln2f = tv(16384, 8192)
        h1b = tv(24576, 16384)

        ccg = TV(stg, 2048, n_ranks * NG * 128)

        pb = [stack.enter_context(nc.psum_tensor(f"pb{i}", [128, 512], F32)) for i in range(8)]

        wload = []
        for dst, src, ncol in [(w_vT, vwT_d, F), (w_gT, gwT_d, 256),
                               (w_qbd, wqbd_d, 256), (w_kbd, wkbd_d, 256)]:
            for kf in range(8):
                wload.append(P.add("G", lambda g, dst=dst, src=src, ncol=ncol, kf=kf:
                                   g.dma_start(out=dslice(dst, ncol * kf, ncol * (kf + 1)),
                                               in_=src[128 * kf:128 * (kf + 1), :]),
                                   dma=True, tag="wload"))
        wload.append(P.add("G", lambda g: g.dma_start(out=rows_b[:, 0:256], in_=gb_d[:, :]), dma=True, tag="ld"))
        wload.append(P.add("G", lambda g: g.dma_start(out=rows_b[:, 256:1280], in_=ob_d[:, :]), dma=True, tag="ld"))
        wload.append(P.add("G", lambda g: g.dma_start(out=rows_b[:, 1280:2304], in_=bpr_d[:, :]), dma=True, tag="ld"))
        wload.append(P.add("G", lambda g: g.dma_start(out=ones_b[:, :], in_=onesb_d[:, :]), dma=True, tag="ld"))
        for kf in range(8):
            wload.append(P.add("G", lambda g, kf=kf: g.dma_start(
                out=lafc[:, kf * 16:(kf + 1) * 16], in_=afcT_d[128 * kf:128 * (kf + 1), :]),
                dma=True, tag="ld"))
        for kf in range(32):
            wload.append(P.add("G", lambda g, kf=kf: g.dma_start(
                out=lapr[:, kf * 16:(kf + 1) * 16], in_=aprT_d[128 * kf:128 * (kf + 1), :]),
                dma=True, tag="ld"))
        wload.append(P.add("G", lambda g: g.dma_start(out=lbfc[:, :], in_=bfcBT_d[:, :]), dma=True, tag="ld"))
        wload.append(P.add("G", lambda g: g.dma_start(out=lbpr[:, :], in_=bprBT_d[:, :]), dma=True, tag="ld"))
        wload.append(P.add("G", lambda g: g.dma_start(out=lbfcr[:, :], in_=bfc_d[:, :]), dma=True, tag="ld"))

        tload = []
        tload.append(P.add("Q", lambda s: s.dma_start(out=tri_m[:], in_=trim_d[:, :]), dma=True, tag="ld"))
        tload.append(P.add("Q", lambda s: s.dma_start(out=blk_m[:], in_=blkm_d[:, :]), dma=True, tag="ld"))
        lamload = []
        for G in range(NG):
            def f(s, G=G):
                srcg = bass.AP(logit_d.tensor, logit_d.offset + 128 * G, [[1, 128], [1, 1]])
                dstg = bass.AP(lam_t[:].tensor, lam_t[:].offset + G, [[NG, 128], [1, 1]])
                return s.dma_start(out=dstg, in_=srcg)
            o = P.add("Q", f, dma=True, tag=f"lgld{G}")
            tload.append(o); lamload.append(o)

        def bcast_load(dst_h, src_ap, n, coloff=0, eng="Q"):
            def f(s):
                sap = bass.AP(src_ap.tensor, src_ap.offset, [[0, 128], [1, n]])
                dap = rawap(dst_h, 0, coloff, [["P", 128], [1, n]])
                return s.dma_start(out=dap, in_=sap)
            return P.add(eng, f, dma=True, tag="bcast")

        tload.append(bcast_load(scr, msel_d, 8, 0))
        tload.append(bcast_load(scr, gap_d, 8, 8))
        tload.append(bcast_load(scr, k128_d, NC, 16))
        tload.append(bcast_load(stg, etlam_d, 8 * C, 0))
        tload.append(bcast_load(stg, etp_d, C, 1024))
        tload.append(bcast_load(stg, etm_d, C, 1152))
        tload.append(bcast_load(stg, etpp_d, B, 1280))

        o_sig = P.add("S", lambda sc: sc.activation(out=lam_t[:], in_=lam_t[:], func=AF.Sigmoid),
                      deps=lamload, tag="sigmoid")
        o_ln = P.add("S", lambda sc: sc.activation(out=L_t[:], in_=lam_t[:], func=AF.Ln),
                     deps=[o_sig], tag="lnlam")
        o_lc = P.add("V", lambda v: v.tensor_scalar(out=Lc_t[:], in0=L_t[:], scalar1=LMIN,
                                                    scalar2=0.0, op0=ALU.max, op1=ALU.add),
                     deps=[o_ln], tag="Lclamp")

        stage = 1536
        prev_exp = []
        all_exp = []
        for G in range(NG):
            Lg = L_t[:, G:G + 1]
            Lcg = Lc_t[:, G:G + 1]
            argdeps = [o_lc] + tload + prev_exp
            a1 = P.add("V", lambda v, Lcg=Lcg: v.tensor_scalar(
                out=rawap(stg, 0, stage, [["P", 128], [1, 8 * C]]),
                in0=rawap(stg, 0, 0, [["P", 128], [1, 8 * C]]),
                scalar1=Lcg, scalar2=TLAM_CAP, op0=ALU.mult, op1=ALU.min),
                deps=argdeps, tag=f"tlam_arg{G}")
            a2 = P.add("V", lambda v, Lg=Lg: v.tensor_scalar(
                out=rawap(stg, 0, stage + 1024, [["P", 128], [1, C]]),
                in0=rawap(stg, 0, 1024, [["P", 128], [1, C]]),
                scalar1=Lg, scalar2=0.0, op0=ALU.mult, op1=ALU.add),
                deps=argdeps, tag=f"tplus_arg{G}")
            a3 = P.add("V", lambda v, Lg=Lg: v.tensor_scalar(
                out=rawap(stg, 0, stage + 1152, [["P", 128], [1, C]]),
                in0=rawap(stg, 0, 1152, [["P", 128], [1, C]]),
                scalar1=Lg, scalar2=math.log(SCL), op0=ALU.mult, op1=ALU.add),
                deps=argdeps, tag=f"tminus_arg{G}")
            a4 = P.add("V", lambda v, Lcg=Lcg: v.tensor_scalar(
                out=rawap(stg, 0, stage + 1280, [["P", 128], [1, B]]),
                in0=rawap(stg, 0, 1280, [["P", 128], [1, B]]),
                scalar1=Lcg, scalar2=math.log(SCL), op0=ALU.mult, op1=ALU.add),
                deps=argdeps, tag=f"tp_arg{G}")
            a5 = P.add("V", lambda v, Lg=Lg: v.tensor_scalar(
                out=rawap(stg, 0, stage + 1296, [["P", 128], [1, 1]]),
                in0=Lg, scalar1=float(C), scalar2=0.0, op0=ALU.mult, op1=ALU.add),
                deps=argdeps, tag=f"lamC_arg{G}")
            a6 = P.add("V", lambda v, Lg=Lg: v.tensor_tensor(
                out=rawap(stg, 0, stage + 1312, [["P", 128], [1, NC]]),
                in0=bass.AP(Lg.tensor, Lg.offset, [[NG, 128], [0, NC]]),
                in1=rawap(scr, 0, 16, [["P", 128], [1, NC]]), op=ALU.mult),
                deps=argdeps, tag=f"tk_arg{G}")
            a7 = P.add("V", lambda v, Lg=Lg: v.tensor_tensor(
                out=rawap(stg, 0, stage + 1312 + NC, [["P", 128], [1, 8]]),
                in0=bass.AP(Lg.tensor, Lg.offset, [[NG, 128], [0, 8]]),
                in1=rawap(scr, 0, 8, [["P", 128], [1, 8]]), op=ALU.mult),
                deps=argdeps, tag=f"wj_arg{G}")
            exp_deps = [a1, a2, a3, a4, a5, a6, a7]
            g_exp = []
            for dst_ap, src_off, n, dname in [
                (tlam[:, G * 8 * C:(G + 1) * 8 * C], stage, 8 * C, "tlam"),
                (tplus[:, G * C:(G + 1) * C], stage + 1024, C, "tplus"),
                (tminfm[:, G * C:(G + 1) * C], stage + 1152, C, "tminus"),
                (tp_t[:, G * B:(G + 1) * B], stage + 1280, B, "tp"),
                (lamC_t[:, G:G + 1], stage + 1296, 1, "lamC"),
                (tk_t[:, G * NC:(G + 1) * NC], stage + 1312, NC, "tk"),
                (wj_t[:, G * 8:(G + 1) * 8], stage + 1312 + NC, 8, "wj"),
            ]:
                g_exp.append(P.add("S", lambda sc, dst_ap=dst_ap, src_off=src_off, n=n:
                                   sc.activation(out=dst_ap,
                                                 in_=rawap(stg, 0, src_off, [["P", 128], [1, n]]),
                                                 func=AF.Exp),
                                   deps=exp_deps, tag=f"exp_{dname}{G}"))
            prev_exp = g_exp
            all_exp += g_exp
            all_exp.append(P.add("V", lambda v, G=G: v.tensor_tensor(
                out=wj_t[:, G * 8:(G + 1) * 8], in0=wj_t[:, G * 8:(G + 1) * 8],
                in1=rawap(scr, 0, 0, [["P", 128], [1, 8]]), op=ALU.mult),
                deps=g_exp, tag=f"wj_msel{G}"))
        o_pz = P.add("V", lambda v: v.memset(
            rawap(Pst, 0, 0, [["P", 128], [1, NG * (NC + 1) * 128]]), 0.0),
            deps=all_exp, tag="Pzero")
        P.emit_block()

        dumg = [None]
        all_spills = []
        o_xload = [None] * NTT
        o_ln0 = [None] * NTT
        o_tr = [[None] * 8 for _ in range(NTT)]
        o_rs_l = [None] * NTT
        o_rec = [None] * (NG * NC)
        ln0f_last_reader = {}
        bankg = {}
        prev_qr_done = []
        prev_macro_tail = []
        rtmp_guard = [None, None]
        qr_sp_guard = {}
        strip_ctr = [0]
        strip_guard = [None, None, None, None]
        vck_guard = [None] * 4
        atq_guard = {}
        amb_guard = [None, None]

        for m in range(NM):
            csb = 0
            o_cs = []
            csdep = list(atq_guard.get(("cs", m - 1), []))
            for which, src in [(0, cos_d), (1, sin_d)]:
                o_cs.append(P.add("Q", lambda s, which=which, src=src, csb=csb, m=m:
                                  s.dma_start(out=dslice(cs_s, csb + which * 512, csb + which * 512 + 512),
                                              in_=bass.AP(src.tensor, src.offset + m * 512,
                                                          [[T, 128], [1, 512]])),
                                  deps=csdep, dma=True, tag=f"cs{m}"))
            for ii in range(4):
                i = 4 * m + ii
                war = [o_ln0[i - 2]] if i >= 2 else []
                o_xload[i] = P.add("Q", lambda s, i=i: s.dma_start(
                    out=x_t[:, (i % 2) * F:(i % 2 + 1) * F],
                    in_=x_in[128 * i:128 * (i + 1), :]), deps=war, dma=True, tag=f"x{i}")
                xa = x_t[:, (i % 2) * F:(i % 2 + 1) * F]
                sb_ = (i % 2) * 8
                stat_war = [o_rs_l[i - 2]] if i >= 2 else []
                o_s1 = P.add("S", lambda sc, xa=xa, sb_=sb_: sc.activation(
                    out=rawap(stg, 0, 2048, [["P", 128], [1, F]]), in_=xa, func=AF.Copy,
                    accum_out=rawap(st4, 0, sb_ + 0, [["P", 128], [1, 1]])),
                    deps=[o_xload[i], dumg[0]] + stat_war, tag=f"st1_{i}")
                dumg[0] = o_s1
                o_s2 = P.add("S", lambda sc, xa=xa, sb_=sb_: sc.activation(
                    out=rawap(stg, 0, 2048, [["P", 128], [1, F]]), in_=xa, func=AF.Square,
                    accum_out=rawap(st4, 0, sb_ + 1, [["P", 128], [1, 1]])),
                    deps=[o_xload[i], dumg[0]] + stat_war, tag=f"st2_{i}")
                dumg[0] = o_s2
                o_v1 = P.add("V", lambda v, i=i, sb_=sb_: v.tensor_scalar(
                    out=mu_s[:, i:i + 1], in0=rawap(st4, 0, sb_ + 0, [["P", 128], [1, 1]]),
                    scalar1=1.0 / F, scalar2=0.0, op0=ALU.mult, op1=ALU.add),
                    deps=[o_s1], tag=f"mu{i}")
                o_v2 = P.add("V", lambda v, sb_=sb_: v.tensor_scalar(
                    out=rawap(st4, 0, sb_ + 2, [["P", 128], [1, 1]]),
                    in0=rawap(st4, 0, sb_ + 1, [["P", 128], [1, 1]]),
                    scalar1=1.0 / F, scalar2=EPS, op0=ALU.mult, op1=ALU.add),
                    deps=[o_s2], tag=f"s2n{i}")
                o_v3 = P.add("V", lambda v, i=i, sb_=sb_: v.tensor_tensor(
                    out=rawap(st4, 0, sb_ + 3, [["P", 128], [1, 1]]),
                    in0=mu_s[:, i:i + 1], in1=mu_s[:, i:i + 1], op=ALU.mult),
                    deps=[o_v1], tag=f"mu2{i}")
                o_v4 = P.add("V", lambda v, sb_=sb_: v.tensor_tensor(
                    out=rawap(st4, 0, sb_ + 4, [["P", 128], [1, 1]]),
                    in0=rawap(st4, 0, sb_ + 2, [["P", 128], [1, 1]]),
                    in1=rawap(st4, 0, sb_ + 3, [["P", 128], [1, 1]]), op=ALU.subtract),
                    deps=[o_v2, o_v3], tag=f"var{i}")
                o_v5 = P.add("V", lambda v, sb_=sb_: v.reciprocal(
                    out=rawap(st4, 0, sb_ + 5, [["P", 128], [1, 1]]),
                    in_=rawap(st4, 0, sb_ + 4, [["P", 128], [1, 1]])),
                    deps=[o_v4], tag=f"rec{i}")
                o_rs = P.add("S", lambda sc, i=i, sb_=sb_: sc.activation(
                    out=rs_s[:, i:i + 1], in_=rawap(st4, 0, sb_ + 5, [["P", 128], [1, 1]]),
                    func=AF.Sqrt), deps=[o_v5], tag=f"rstd{i}")
                o_rs_l[i] = o_rs
                war2 = o_tr[i - 2] if i >= 2 else []
                o_ln0[i] = P.add("V", lambda v, i=i: v.scalar_tensor_tensor(
                    out=lntm[:, (i % 2) * F:(i % 2 + 1) * F],
                    in0=x_t[:, (i % 2) * F:(i % 2 + 1) * F],
                    scalar=mu_s[:, i:i + 1],
                    in1=bass.AP(rs_s[:].tensor, rs_s[:].offset + i, [[NTT, 128], [0, F]]),
                    op0=ALU.subtract, op1=ALU.mult),
                    deps=[o_rs] + war2, tag=f"ln0_{i}")
                warf = [ln0f_last_reader.get(m - 1)] if m >= 1 else []
                for fb in range(8):
                    o_tr[i][fb] = add_tr(lambda e, i=i, ii=ii, fb=fb: e.dma_start(
                        out=dslice(ln0f, fb * 512 + ii * 128, fb * 512 + ii * 128 + 128),
                        in_=lntm[:, (i % 2) * F + fb * 128:(i % 2) * F + fb * 128 + 128],
                        transpose=True),
                        deps=[o_ln0[i]] + warf, tag=f"tr{i}_{fb}")

            tr_deps = [o_tr[4 * m + ii][fb] for ii in range(4) for fb in range(8)]

            for qk, wsrc, rot in [(0, qT_d, qrot), (1, kT_d, krot)]:
                for mp in range(4):
                    strips = {}
                    for Mt in (mp, mp + 4):
                        slot = strip_ctr[0] % 4
                        strip_ctr[0] += 1
                        sg = strip_guard[slot]
                        o_st = P.add("G", lambda g, wsrc=wsrc, Mt=Mt, slot=slot: g.dma_start(
                            out=dslice(wqk_s, slot * 1024, slot * 1024 + 1024),
                            in_=bass.AP(wsrc.tensor, wsrc.offset + Mt * 128,
                                        [[F, 128], [128 * F, 8], [1, 128]])),
                            deps=[sg], dma=True, tag=f"wst{qk}{Mt}m{m}")
                        strips[Mt] = (slot, o_st)
                    pi = (qk * 4 + mp)
                    be_i, bo_i = pi % 2, 2 + pi % 2
                    bank_e, bank_o = pb[be_i], pb[bo_i]
                    mm_e = mm_o = None
                    mm_by = {}
                    for Mt, bank, bi in [(mp, bank_e, be_i), (mp + 4, bank_o, bo_i)]:
                        slot, o_st = strips[Mt]
                        for kf in range(8):
                            dep = (tr_deps + bankg.get(bi, []) + [o_st]) if kf == 0 else []
                            o_mm = P.add("T", lambda t, kf=kf, bank=bank, slot=slot:
                                         t.matmul(bank[:, :],
                                                  dslice(wqk_s, slot * 1024 + kf * 128,
                                                         slot * 1024 + kf * 128 + 128),
                                                  dslice(ln0f, kf * 512, (kf + 1) * 512),
                                                  start=(kf == 0), stop=(kf == 7)),
                                         deps=dep, tag=f"qk{qk}mm{Mt}k{kf}m{m}")
                        mm_by[Mt] = o_mm
                        strip_guard[slot] = o_mm
                    mm_e, mm_o = mm_by[mp], mm_by[mp + 4]
                    wr0 = [rtmp_guard[0]] if rtmp_guard[0] else []
                    wr1 = [rtmp_guard[1]] if rtmp_guard[1] else []
                    o_p1 = P.add("V", lambda v, bank_e=bank_e, csb=csb: v.tensor_tensor(
                        out=dslice(rtmp, 0, 512), in0=bank_e[:, :],
                        in1=dslice(cs_s, csb, csb + 512), op=ALU.mult),
                        deps=[mm_e] + wr0 + o_cs, tag="rp1")
                    o_p2 = P.add("V", lambda v, bank_o=bank_o, csb=csb: v.tensor_tensor(
                        out=dslice(rtmp, 512, 1024), in0=bank_o[:, :],
                        in1=dslice(cs_s, csb + 512, csb + 1024), op=ALU.mult),
                        deps=[mm_o] + wr0 + o_cs, tag="rp2")
                    o_p3 = P.add("V", lambda v, bank_e=bank_e, csb=csb: v.tensor_tensor(
                        out=dslice(rtmp, 1024, 1536), in0=bank_e[:, :],
                        in1=dslice(cs_s, csb + 512, csb + 1024), op=ALU.mult),
                        deps=[mm_e] + wr1 + o_cs, tag="rp3")
                    o_p4 = P.add("V", lambda v, bank_o=bank_o, csb=csb: v.tensor_tensor(
                        out=dslice(rtmp, 1536, 2048), in0=bank_o[:, :],
                        in1=dslice(cs_s, csb, csb + 512), op=ALU.mult),
                        deps=[mm_o] + wr1 + o_cs, tag="rp4")
                    o_c1 = P.add("G", lambda g, rot=rot, mp=mp: g.tensor_tensor(
                        out=dslice(rot, mp * 512, (mp + 1) * 512), in0=dslice(rtmp, 0, 512),
                        in1=dslice(rtmp, 512, 1024), op=ALU.subtract),
                        deps=[o_p1, o_p2] + prev_qr_done, tag="rc1")
                    o_c2 = P.add("G", lambda g, rot=rot, mp=mp: g.tensor_tensor(
                        out=dslice(rot, (mp + 4) * 512, (mp + 5) * 512), in0=dslice(rtmp, 1024, 1536),
                        in1=dslice(rtmp, 1536, 2048), op=ALU.add),
                        deps=[o_p3, o_p4] + prev_qr_done, tag="rc2")
                    rtmp_guard[0] = o_c1
                    rtmp_guard[1] = o_c2
                    bankg[be_i] = [o_p1, o_p3]
                    bankg[bo_i] = [o_p2, o_p4]
            rope_all = [rtmp_guard[0], rtmp_guard[1]]

            o_vev = {}
            vparity = m % 2
            for ii in range(4):
                i = 4 * m + ii
                for nt in range(2):
                    bi = 4 + nt
                    bank = pb[bi]
                    for kf in range(8):
                        dep = (wload + tr_deps + bankg.get(bi, []) +
                               (vck_guard[ii] or [])) if kf == 0 else []
                        o_mm = P.add("T", lambda t, kf=kf, ii=ii, nt=nt, bank=bank:
                                     t.matmul(bank[:, :],
                                              dslice(ln0f, kf * 512 + ii * 128, kf * 512 + ii * 128 + 128),
                                              dslice(w_vT, kf * F + nt * 512, kf * F + nt * 512 + 512),
                                              start=(kf == 0), stop=(kf == 7)),
                                     deps=dep, tag=f"vmm{i}_{nt}")
                    o_ev = P.add("S", lambda sc, ii=ii, nt=nt, bank=bank: sc.copy(
                        out=dslice(vck, ii * 1024 + nt * 512, ii * 1024 + nt * 512 + 512),
                        in_=bank[:, :]),
                        deps=[o_mm], tag=f"vev{i}_{nt}")
                    o_vev[(ii, nt)] = o_ev
                    bankg[bi] = [o_ev]
                o_vev[(ii, "sp")] = P.add("Q", lambda s, i=i, ii=ii: s.dma_start(
                    out=v_d[128 * i:128 * (i + 1), :],
                    in_=dslice(vck, ii * 1024, ii * 1024 + 1024)),
                    deps=[o_vev[(ii, 0)], o_vev[(ii, 1)]], dma=True, tag=f"vsp{i}")
                all_spills.append(o_vev[(ii, "sp")])

            o_gev = []
            for G in range(NG):
                bank = pb[6]
                for kf in range(8):
                    dep = (wload + tr_deps + bankg.get(6, []) + prev_macro_tail) if kf == 0 else []
                    o_mm = P.add("T", lambda t, kf=kf, G=G, bank=bank:
                                 t.matmul(bank[:, :],
                                          dslice(w_gT, kf * 256 + G * 128, kf * 256 + G * 128 + 128),
                                          dslice(ln0f, kf * 512, (kf + 1) * 512),
                                          start=(kf == 0), stop=False),
                                 deps=dep, tag=f"gmm{G}k{kf}")
                o_mmb = P.add("T", lambda t, G=G, bank=bank: t.matmul(
                    bank[:, :], rawap(rows_b, 0, G * 128, [[4096, 1], [1, 128]]),
                    ones_b[:, :], start=False, stop=True), deps=[], tag=f"gbias{G}")
                o_ev = P.add("S", lambda sc, G=G, bank=bank: sc.activation(
                    out=dslice(g_fm, G * 512, (G + 1) * 512), in_=bank[:, :], func=AF.Sigmoid),
                    deps=[o_mmb], tag=f"gsig{G}")
                o_gev.append(o_ev)
                bankg[6] = [o_ev]

            o_qr_ev = []
            qrp = m % 2
            for qk, wbd, rot in [(0, w_qbd, qrot), (1, w_kbd, krot)]:
                for G in range(NG):
                    bank = pb[7]
                    for kf in range(8):
                        dep = (wload + rope_all + bankg.get(7, []) + prev_macro_tail) if kf == 0 else []
                        o_mm = P.add("T", lambda t, kf=kf, G=G, bank=bank, wbd=wbd, rot=rot:
                                     t.matmul(bank[:, :],
                                              dslice(wbd, kf * 256 + G * 128, kf * 256 + G * 128 + 128),
                                              dslice(rot, kf * 512, (kf + 1) * 512),
                                              start=(kf == 0), stop=(kf == 7)),
                                     deps=dep, tag=f"qr{qk}g{G}k{kf}")
                    if qk == 0:
                        o_ev = P.add("S", lambda sc, G=G, bank=bank, qrp=qrp: sc.copy(
                            out=dslice(qr_mb, qrp * 1024 + G * 512, qrp * 1024 + G * 512 + 512),
                            in_=bank[:, :]),
                            deps=[o_mm] + qr_sp_guard.get(qrp, []), tag=f"qrev{G}")
                    else:
                        o_ev = P.add("S", lambda sc, G=G, bank=bank: sc.copy(
                            out=dslice(kr_fm, G * 512, (G + 1) * 512), in_=bank[:, :]),
                            deps=[o_mm], tag=f"krev{G}")
                    o_qr_ev.append(o_ev)
                    bankg[7] = [o_ev]
            prev_qr_done = [o_qr_ev[-1]]
            ln0f_last_reader[m] = o_qr_ev[-1]
            qr_sp_guard[qrp] = []
            for G in range(NG):
                qr_sp_guard[qrp].append(P.add("Q", lambda s, G=G, m=m, qrp=qrp: s.dma_start(
                    out=bass.AP(qr_d.tensor, qr_d.offset + G * T + m * 512, [[NG * T, 128], [1, 512]]),
                    in_=dslice(qr_mb, qrp * 1024 + G * 512, qrp * 1024 + G * 512 + 512)),
                    deps=[o_qr_ev[G]], dma=True, tag=f"qrsp{G}m{m}"))
                all_spills.append(qr_sp_guard[qrp][-1])

            atp = m % 2
            o_at_sp = []
            for G in range(NG):
                dep = [o_gev[G], o_qr_ev[2 + G]]
                if m >= 1:
                    dep += [atq_guard.get(("afm", m - 1))]
                o_a1 = P.add("V", lambda v, G=G: v.tensor_tensor(
                    out=dslice(a_fm, G * 512, (G + 1) * 512), in0=dslice(g_fm, G * 512, (G + 1) * 512),
                    in1=dslice(kr_fm, G * 512, (G + 1) * 512), op=ALU.mult), deps=dep, tag=f"a{G}")
                o_a2 = P.add("V", lambda v, G=G, atp=atp: v.tensor_tensor(
                    out=rawap(at_mb, 0, atp * 1024 + G * 512, [["P", 128], [B, 32], [1, B]]),
                    in0=rawap(a_fm, 0, G * 512, [["P", 128], [B, 32], [1, B]]),
                    in1=rawap(tp_t, 0, G * B, [["P", 128], [0, 32], [1, B]]),
                    op=ALU.mult), deps=[o_a1] + ([atq_guard.get(("at", m - 2))] if m >= 2 else []),
                    tag=f"at{G}")
                o_a3 = P.add("G", lambda g, G=G: g.tensor_tensor(
                    out=rawap(amf, 0, G * 512, [["P", 128], [C, 4], [1, C]]),
                    in0=rawap(a_fm, 0, G * 512, [["P", 128], [C, 4], [1, C]]),
                    in1=rawap(tminfm, 0, G * C, [["P", 128], [0, 4], [1, C]]),
                    op=ALU.mult), deps=[o_a1] + ([atq_guard.get(("amf", m - 1))] if m >= 1 else []),
                    tag=f"amf{G}")
                o_sp = P.add("Q", lambda s, G=G, m=m, atp=atp: s.dma_start(
                    out=bass.AP(at_d.tensor, at_d.offset + G * T + m * 512, [[NG * T, 128], [1, 512]]),
                    in_=dslice(at_mb, atp * 1024 + G * 512, atp * 1024 + G * 512 + 512)),
                    deps=[o_a2], dma=True, tag=f"atsp{G}m{m}")
                o_at_sp.append(o_sp)
                all_spills.append(o_sp)
                atq_guard[("afm_g", m, G)] = (o_a2, o_a3)
            atq_guard[("afm", m)] = atq_guard[("afm_g", m, NG - 1)][0]
            atq_guard[("at", m)] = o_at_sp[-1]

            amp = m % 2
            o_amt = []
            for G in range(NG):
                for tb in range(4):
                    o_t = add_tr(lambda s, G=G, tb=tb, amp=amp: s.dma_start(
                        out=dslice(am_mb, amp * 1024 + tb * 256 + G * 128,
                                   amp * 1024 + tb * 256 + G * 128 + 128),
                        in_=dslice(amf, G * 512 + tb * 128, G * 512 + tb * 128 + 128),
                        transpose=True),
                        deps=[atq_guard[("afm_g", m, G)][1]] +
                             ([amb_guard[amp]] if amb_guard[amp] else []),
                        tag=f"amt{G}_{tb}")
                    o_amt.append(o_t)
            atq_guard[("amf", m)] = o_amt[-1]

            for ii in range(4):
                k = 4 * m + ii
                for G in range(NG):
                    bi = 6 + G
                    bank = pb[bi]
                    mm_list = []
                    for p4 in range(4):
                        dep = (o_amt + [o_vev[(ii, 0)], o_vev[(ii, 1)]] +
                           bankg.get(bi, [])) if p4 == 0 else []
                        mm = P.add("T", lambda t, ii=ii, G=G, p4=p4, bank=bank, amp=amp: t.matmul(
                            bank[32 * p4:32 * p4 + 32, 0:128],
                            dslice(am_mb, amp * 1024 + ii * 256 + G * 128 + 32 * p4,
                                   amp * 1024 + ii * 256 + G * 128 + 32 * p4 + 32),
                            dslice(vck, ii * 1024 + G * 512 + 128 * p4,
                                   ii * 1024 + G * 512 + 128 * p4 + 128),
                            start=True, stop=True, tile_position=(0, 32 * p4)),
                            deps=dep, tag=f"U{k}g{G}p{p4}")
                        mm_list.append(mm)
                    o_rec[G * NC + k] = P.add("V", lambda v, k=k, G=G, bank=bank: v.scalar_tensor_tensor(
                        out=rawap(Pst, 0, (G * (NC + 1) + k + 1) * 128, [["P", 128], [1, 128]]),
                        in0=rawap(Pst, 0, (G * (NC + 1) + k) * 128, [["P", 128], [1, 128]]),
                        scalar=lamC_t[:, G:G + 1],
                        in1=bank[:, 0:128], op0=ALU.mult, op1=ALU.add),
                        deps=mm_list + ([o_rec[G * NC + k - 1]] if k >= 1 else []),
                        tag=f"rec{k}g{G}")
                    bankg[bi] = [o_rec[G * NC + k]]
                vck_guard[ii] = [o_rec[1 * NC + k], o_vev[(ii, "sp")]]
            amb_guard[amp] = o_rec[1 * NC + 4 * m + 3]
            atq_guard[("cs", m)] = [rtmp_guard[0], rtmp_guard[1]]
            prev_macro_tail = [o_rec[0 * NC + 4 * m + 3], o_rec[1 * NC + 4 * m + 3]]

        P.emit_block()
        if max_phase < 2:
            return nc

        o_pay = []
        for G in range(NG):
            o_pay.append(P.add("Q", lambda s, G=G: s.dma_start(
                out=cc_in.ap()[G * 128:(G + 1) * 128, :],
                in_=rawap(Pst, 0, (G * (NC + 1) + NC) * 128, [["P", 128], [1, 128]])),
                dma=True, tag=f"pay{G}"))
        o_cc = P.add("G", lambda g: g.collective_compute(
            "AllGather", ALU.bypass, replica_groups=[list(range(n_ranks))],
            ins=[cc_in.ap().opt()], outs=[cc_out.ap().opt()]),
            deps=o_pay, tag="allgather", coll=True)
        o_ccl = P.add("Q", lambda s: s.dma_start(
            out=ccg.sl(0, n_ranks * NG * 128),
            in_=bass.AP(cc_out.ap().tensor, cc_out.ap().offset,
                        [[128, 128], [128 * 128, n_ranks * NG], [1, 128]])),
            deps=[o_cc], dma=True, tag="ccload")
        o_fin = [None] * (NG * NC)
        for G in range(NG):
            o_z = P.add("V", lambda v, G=G: v.memset(
                rawap(Msta, 0, G * 256, [["P", 128], [1, 128]]), 0.0), tag=f"msz{G}")
            prev = o_z
            cur = 0
            for j in range(n_ranks):
                prev = P.add("V", lambda v, G=G, j=j, cur=cur: v.scalar_tensor_tensor(
                    out=rawap(Msta, 0, G * 256 + (1 - cur) * 128, [["P", 128], [1, 128]]),
                    in0=ccg.ap(0, (j * NG + G) * 128, [["P", 128], [1, 128]]),
                    scalar=wj_t[:, G * 8 + j:G * 8 + j + 1],
                    in1=rawap(Msta, 0, G * 256 + cur * 128, [["P", 128], [1, 128]]),
                    op0=ALU.mult, op1=ALU.add),
                    deps=[prev, o_ccl], tag=f"ms{G}_{j}")
                cur = 1 - cur
            for k in range(NC):
                o_fin[G * NC + k] = P.add("V", lambda v, G=G, k=k, cur=cur: v.scalar_tensor_tensor(
                    out=rawap(Pst, 0, (G * (NC + 1) + k) * 128, [["P", 128], [1, 128]]),
                    in0=rawap(Msta, 0, G * 256 + cur * 128, [["P", 128], [1, 128]]),
                    scalar=tk_t[:, G * NC + k:G * NC + k + 1],
                    in1=rawap(Pst, 0, (G * (NC + 1) + k) * 128, [["P", 128], [1, 128]]),
                    op0=ALU.mult, op1=ALU.add),
                    deps=[prev] + ([o_pay[G]] if k == 0 else []), tag=f"fin{G}_{k}")
        P.emit_block()
        if max_phase < 3:
            return nc

        o_zbd = [
            P.add("V", lambda v: v.memset(abd.sl(0, 2 * NG * 1024), 0.0), tag="zabd"),
            P.add("V", lambda v: v.memset(vbd.sl(0, 2 * NG * 4096), 0.0), tag="zvbd"),
            P.add("V", lambda v: v.memset(mkbd.sl(0, 8192), 0.0), tag="zmkbd"),
        ]
        o_ysp = [None] * NC
        pc = {}
        for k in range(NC):
            kb = k % 2
            o_qrl = P.add("Q", lambda s, k=k, kb=kb: s.dma_start(
                out=dslice(qrc, kb * 256, kb * 256 + 256),
                in_=bass.AP(qr_d.tensor, qr_d.offset + k * C,
                            [[NG * T, 128], [T, NG], [1, C]])),
                deps=pc.get(("qtqp", k - 2), []) + (all_spills if k == 0 else []),
                dma=True, tag=f"qrl{k}")
            if k % 4 == 0:
                bnum = k // 4
                bpar = bnum % 2
                nk4 = min(4, NC - k)
                mkb_build = []
                war_mk = ([pc.get(("ymm", 4 * (bnum - 2) + 3))] if bnum >= 2
                          else list(o_zbd))
                for G in range(NG):
                    findeps = [o_fin[G * NC + k + kk4] for kk4 in range(nk4)]
                    for p4 in range(4):
                        eng = "V"
                        def fmk(e, G=G, p4=p4, k=k, bpar=bpar, nk4=nk4):
                            s_ap = rawap(Pst, 32 * p4, (G * (NC + 1) + k) * 128,
                                         [["P", 32], [128, nk4], [1, 128]])
                            t_ap = mkbd.ap(32 * p4, bpar * 4096 + G * 512 + 128 * p4,
                                           [["P", 32], [1024, nk4], [1, 128]])
                            m_ap = rawap(blk_m, 32 * p4, 0,
                                         [["P", 32], [0, nk4], [1, 128]])
                            return e.tensor_tensor(out=t_ap, in0=s_ap, in1=m_ap,
                                                   op=ALU.mult)
                        mkb_build.append(P.add(eng, fmk,
                            deps=findeps + war_mk, tag=f"mkbd{k}g{G}p{p4}"))
                pc[("mkbd", bnum)] = mkb_build
            o_mkb = pc[("mkbd", k // 4)]
            bpar_k = (k // 4) % 2
            o_qt = []
            o_qp = []
            for G in range(NG):
                dep = [o_qrl] + ([pc[("smm", k - 1, G)]] if k >= 1 else [])
                o_qt.append(P.add("V", lambda v, G=G, kb=kb: v.tensor_tensor(
                    out=rawap(qt_b, 0, G * 8 * C, [["P", 128], [C, 8], [1, C]]),
                    in0=rawap(qrc, 0, kb * 256 + G * C, [["P", 128], [0, 8], [1, C]]),
                    in1=rawap(tlam, 0, G * 8 * C, [["P", 128], [C, 8], [1, C]]),
                    op=ALU.mult), deps=dep, tag=f"qt{k}g{G}"))
                o_qp.append(P.add("G", lambda g, G=G, kb=kb: g.tensor_tensor(
                    out=dslice(qp_b, G * C, (G + 1) * C),
                    in0=dslice(qrc, kb * 256 + G * C, kb * 256 + (G + 1) * C),
                    in1=tplus[:, G * C:(G + 1) * C], op=ALU.mult),
                    deps=[o_qrl] + ([pc[("ymm", k - 1)]] if k >= 1 else []), tag=f"qp{k}g{G}"))
            pc[("qtqp", k)] = [o_qt[-1], o_qp[-1]]
            o_bd = []
            war_bd = ([pc[("ymm", k - 2)]] if k >= 2 else []) + \
                (all_spills + o_zbd if k == 0 else [])
            for G in range(NG):
                for hl in range(8):
                    def fa(s, G=G, hl=hl, k=k, kb=kb):
                        t_ap = rawap(abd, 16 * hl, kb * 2048 + G * 1024 + 16 * hl,
                                     [["P", 16], [128, 8], [1, 16]])
                        s_ap = bass.AP(at_d.tensor, at_d.offset + (16 * hl) * NG * T + G * T + k * C,
                                       [[NG * T, 16], [16, 8], [1, 16]])
                        return s.dma_start(out=t_ap, in_=s_ap)
                    if "bd" in STUB:
                        def fa(s, G=G, hl=hl, k=k, kb=kb):
                            t_ap = rawap(abd, 16 * hl, kb * 2048 + G * 1024,
                                         [["P", 16], [1, 128]])
                            s_ap = bass.AP(at_d.tensor, at_d.offset + (16 * hl) * NG * T,
                                           [[NG * T, 16], [1, 128]])
                            return s.dma_start(out=t_ap, in_=s_ap)
                    o_bd.append(P.add("Q", fa, deps=war_bd, dma=True, tag=f"abd{k}g{G}h{hl}"))
                    def fv(s, G=G, hl=hl, k=k, kb=kb):
                        t_ap = rawap(vbd, 16 * hl, kb * 8192 + G * 4096 + 64 * hl,
                                     [["P", 16], [512, 8], [1, 64]])
                        s_ap = bass.AP(v_d.tensor, v_d.offset + (k * C) * F + G * 512 + 64 * hl,
                                       [[F, 16], [16 * F, 8], [1, 64]])
                        return s.dma_start(out=t_ap, in_=s_ap)
                    if "bd" in STUB:
                        def fv(s, G=G, hl=hl, k=k, kb=kb):
                            t_ap = rawap(vbd, 16 * hl, kb * 8192 + G * 4096,
                                         [["P", 16], [1, 512]])
                            s_ap = bass.AP(v_d.tensor, v_d.offset + (k * C) * F,
                                           [[F, 16], [1, 512]])
                            return s.dma_start(out=t_ap, in_=s_ap)
                    o_bd.append(P.add("Q", fv, deps=war_bd, dma=True, tag=f"vbd{k}g{G}h{hl}"))
            o_smm = {}
            for G in range(NG):
                last = None
                for j in range(8):
                    bank = pb[G * 2 + j // 4]
                    dep = (o_bd + o_qt +
                           ([pc[("sev", k - 1, G)]] if k >= 1 else [])) if j == 0 else o_bd
                    if "smm" in STUB:
                        last = P.add("T", lambda t, G=G, j=j, kb=kb, bank=bank:
                                     t.matmul(bank[:, (j % 4) * 128:(j % 4) * 128 + C],
                                              dslice(abd, kb * 2048 + G * 1024 + j * 128,
                                                     kb * 2048 + G * 1024 + j * 128 + 128),
                                              dslice(qt_b, G * 8 * C + j * C,
                                                     G * 8 * C + (j + 1) * C),
                                              start=True, stop=True),
                                     deps=dep, tag=f"smm{k}g{G}j{j}")
                    else:
                        last = P.add("T", lambda t, G=G, j=j, kb=kb, bank=bank:
                                     t.matmul(bank[:, (j % 4) * 128 + B * j:(j % 4) * 128 + C],
                                              dslice(abd, kb * 2048 + G * 1024 + j * 128,
                                                     kb * 2048 + G * 1024 + j * 128 + 128),
                                              dslice(qt_b, G * 8 * C + j * C + B * j,
                                                     G * 8 * C + (j + 1) * C),
                                              start=True, stop=True),
                                     deps=dep, tag=f"smm{k}g{G}j{j}")
                o_smm[G] = last
                pc[("smm", k, G)] = last
            o_sev = {}
            for G in range(NG):
                evs = []
                for half in range(2):
                    bank = pb[G * 2 + half]
                    evs.append(P.add("V", lambda v, G=G, half=half, bank=bank: v.tensor_tensor(
                        out=dslice(ssb, G * 8 * C + half * 512, G * 8 * C + half * 512 + 512),
                        in0=bank[:, :], in1=tri_m[:, half * 512:half * 512 + 512],
                        op=ALU.mult),
                        deps=[o_smm[G]] + ([pc[("ymm", k - 1)]] if k >= 1 else []),
                        tag=f"sev{k}g{G}h{half}"))
                o_sev[G] = evs
                pc[("sev", k, G)] = evs[-1]
            o_ymm = []
            for G in range(NG):
                bank = pb[6 + G]
                last = None
                for j in range(8):
                    dep = (o_sev[G] + o_bd +
                           ([pc[("yev", k - 1, G)]] if k >= 1 else [])) if j == 0 else []
                    last = P.add("T", lambda t, G=G, j=j, kb=kb, bank=bank:
                                 t.matmul(bank[:, :],
                                          dslice(ssb, G * 8 * C + j * 128, G * 8 * C + j * 128 + 128),
                                          dslice(vbd, kb * 8192 + G * 4096 + j * 512,
                                                 kb * 8192 + G * 4096 + (j + 1) * 512),
                                          start=(j == 0), stop=False),
                                 deps=dep, tag=f"yin{k}g{G}j{j}")
                last = P.add("T", lambda t, G=G, bank=bank, k=k, bpar_k=bpar_k:
                             t.matmul(bank[:, :],
                                      qp_b.sl(G * C, (G + 1) * C),
                                      mkbd.sl(bpar_k * 4096 + (k % 4) * 1024 + G * 512,
                                              bpar_k * 4096 + (k % 4) * 1024 + G * 512 + 512),
                                      start=False, stop=True),
                             deps=[o_qp[G]] + o_mkb, tag=f"yie{k}g{G}")
                o_ymm.append(last)
            pc[("ymm", k)] = o_ymm[-1]
            o_yev = []
            for G in range(NG):
                o_yev.append(P.add("S", lambda sc, G=G: sc.copy(
                    out=lntm[:, G * 512:(G + 1) * 512], in_=pb[6 + G][:, :]),
                    deps=[o_ymm[G]] + (pc[("ytr", k - 1)] if k >= 1 else []),
                    tag=f"yev{k}g{G}"))
                pc[("yev", k, G)] = o_yev[-1]
            o_ytr = []
            for fb in range(8):
                o_ytr.append(add_tr(lambda e, fb=fb: e.dma_start(
                    out=yfmc[:, fb * 128:fb * 128 + 128],
                    in_=lntm[:, fb * 128:fb * 128 + 128], transpose=True),
                    deps=o_yev + ([o_ysp[k - 1]] if k >= 1 else []),
                    tag=f"ytr{k}_{fb}"))
            pc[("ytr", k)] = o_ytr
            o_ysp[k] = P.add("Q", lambda s, k=k: s.dma_start(
                out=bass.AP(yfm_d.tensor, yfm_d.offset + k * 1024, [[T * 8, 128], [1, 1024]]),
                in_=yfmc[:, 0:1024]), deps=o_ytr, dma=True, tag=f"ysp{k}")
        P.emit_block()
        if max_phase < 4:
            return nc

        o_wol = []
        for kf in range(8):
            o_wol.append(P.add("G", lambda g, kf=kf: g.dma_start(
                out=dslice(w_oT, F * kf, F * (kf + 1)),
                in_=owT_d[128 * kf:128 * (kf + 1), :]), dma=True, tag=f"wol{kf}"))
        o_lng = bcast_load(lng_r, lng_d, F)
        o_lnb = bcast_load(lnb_r, lnb_d, F)
        o_scale = {}
        o_wq_last = {}
        fqs = tv(40960, 4096)

        def fq_gen():
            prev_tail = None
            for wname, src_d, ntile, ncol, rng in [("fc", wfcT_d, 8, FF, "G"),
                                                   ("pr", wprT_d, 32, F, "Q")]:
                prev_ld = prev_tail
                o_acc_last = None
                for kt in range(ntile):
                    o_ld = P.add(rng, lambda s, src_d=src_d, kt=kt, ncol=ncol: s.dma_start(
                        out=rawap(Pst, 0, 4096, [["P", 128], [1, ncol]]),
                        in_=src_d[128 * kt:128 * (kt + 1), :]),
                        deps=([prev_ld] if prev_ld else []),
                        dma=True, tag=f"wld{wname}{kt}")
                    yield
                    o_m = P.add("V", lambda v, ncol=ncol: v.tensor_reduce(
                        out=rawap(st4, 0, 14, [["P", 128], [1, 1]]),
                        in_=rawap(Pst, 0, 4096, [["P", 128], [1, ncol]]),
                        axis=mybir.AxisListType.X, op=ALU.max, apply_absolute_value=True),
                        deps=[o_ld], tag=f"wmx{wname}{kt}")
                    yield
                    o_acc = P.add("V", lambda v, kt=kt: v.tensor_tensor(
                        out=rawap(st4, 0, 15, [["P", 128], [1, 1]]),
                        in0=(rawap(st4, 0, 15, [["P", 128], [1, 1]]) if kt > 0
                             else rawap(st4, 0, 14, [["P", 128], [1, 1]])),
                        in1=rawap(st4, 0, 14, [["P", 128], [1, 1]]), op=ALU.max),
                        deps=[o_m], tag=f"wacc{wname}{kt}")
                    yield
                    prev_ld = o_acc
                    o_acc_last = o_acc
                o_tp1 = P.add("Q", lambda s: s.dma_start(
                    out=rawap(scr, 0, 64, [["P", 1], [1, 128]]),
                    in_=rawap(st4, 0, 15, [["P", 128], [1, 1]])),
                    deps=[o_acc_last], dma=True, tag=f"wtr{wname}")
                yield
                o_gm = P.add("V", lambda v: v.tensor_reduce(
                    out=rawap(scr, 0, 192, [["P", 1], [1, 1]]),
                    in_=rawap(scr, 0, 64, [["P", 1], [1, 128]]),
                    axis=mybir.AxisListType.X, op=ALU.max), deps=[o_tp1], tag=f"wgm{wname}")
                o_sc = P.add("V", lambda v: v.tensor_scalar(
                    out=rawap(scr, 0, 193, [["P", 1], [1, 1]]), in0=rawap(scr, 0, 192, [["P", 1], [1, 1]]),
                    scalar1=1.0 / 127.0, scalar2=0.0, op0=ALU.mult, op1=ALU.add), deps=[o_gm], tag=f"wsc{wname}")
                o_rc = P.add("V", lambda v: v.reciprocal(
                    out=rawap(scr, 0, 194, [["P", 1], [1, 1]]), in_=rawap(scr, 0, 193, [["P", 1], [1, 1]])),
                    deps=[o_sc], tag=f"wrc{wname}")
                yield
                o_b1 = P.add("Q", lambda s: s.dma_start(
                    out=bass.AP(srow_d.ap().tensor, srow_d.ap().offset, [[1, 1], [1, 2]]),
                    in_=rawap(scr, 0, 193, [["P", 1], [1, 2]])), deps=[o_rc], dma=True, tag=f"wb1{wname}")
                yield
                col = 200 if wname == "fc" else 204
                o_b2 = P.add("Q", lambda s, col=col: s.dma_start(
                    out=rawap(scr, 0, col, [["P", 128], [1, 2]]),
                    in_=bass.AP(srow_d.ap().tensor, srow_d.ap().offset, [[0, 128], [1, 2]])),
                    deps=[o_b1], dma=True, tag=f"wb2{wname}")
                yield
                o_scale[wname] = o_b2
                prev_tail = o_b2

            prev = o_scale["pr"]
            for wname, src_d, dst_d, ntile, ncol, rng in [("fc", wfcT_d, wqf_d, 8, FF, "G"),
                                                          ("pr", wprT_d, wqp_d, 32, F, "Q")]:
                col = 200 if wname == "fc" else 204
                nh = ncol // 1024 if ncol > 1024 else 1
                hw_ = ncol // nh
                dpitch = ntile * ncol
                for kt in range(ntile):
                    for hh in range(nh):
                        o_ld = P.add(rng, lambda s, src_d=src_d, kt=kt, hh=hh, hw_=hw_: s.dma_start(
                            out=rawap(Pst, 0, 4096, [["P", 128], [1, hw_]]),
                            in_=src_d[128 * kt:128 * (kt + 1), hh * hw_:(hh + 1) * hw_]),
                            deps=([prev] if prev else []),
                            dma=True, tag=f"wl2{wname}{kt}_{hh}")
                        yield
                        o_q1 = P.add("V", lambda v, hw_=hw_, col=col: v.tensor_scalar(
                            out=ki32[:, 0:hw_],
                            in0=rawap(Pst, 0, 4096, [["P", 128], [1, hw_]]),
                            scalar1=rawap(scr, 0, col + 1, [["P", 128], [1, 1]]), scalar2=0.0,
                            op0=ALU.mult, op1=ALU.add), deps=[o_ld, o_scale[wname]],
                            tag=f"q1{wname}{kt}_{hh}")
                        yield
                        o_q4 = P.add("V", lambda v, hw_=hw_, hh=hh, col=col: v.tensor_scalar(
                            out=rawap(fqs, 0, hh * hw_, [["P", 128], [1, hw_]]),
                            in0=ki32[:, 0:hw_],
                            scalar1=-128.0, scalar2=rawap(scr, 0, col, [["P", 128], [1, 1]]),
                            op0=ALU.max, op1=ALU.mult), deps=[o_q1], tag=f"q4{wname}{kt}_{hh}")
                        yield
                        prev = o_q4
                    o_sp = P.add(rng, lambda s, dst_d=dst_d, kt=kt, ncol=ncol, dpitch=dpitch:
                                 s.dma_start(
                                     out=bass.AP(dst_d.tensor, dst_d.offset + kt * ncol,
                                                 [[dpitch, 128], [1, ncol]]),
                                     in_=rawap(fqs, 0, 0, [["P", 128], [1, ncol]])),
                                 deps=[prev], dma=True, tag=f"wsp{wname}{kt}")
                    yield
                    prev = o_sp
                o_wq_last[wname] = prev

        _fq = fq_gen()

        def fq_drip(n):
            for _ in range(n):
                try:
                    next(_fq)
                except StopIteration:
                    return

        o_xn = [None] * NTT
        prev_b4 = {}
        for i in range(NTT):
            par = i % 2
            sb = 16 * par
            s_ln0 = 2048 * par
            s_hr = 2048 * par + 1024
            o_x2 = P.add("Q", lambda s, i=i, par=par: s.dma_start(
                out=x_t[:, par * F:(par + 1) * F],
                in_=x_in[128 * i:128 * (i + 1), :]),
                deps=[o_xn[i - 2]] if i >= 2 else [], dma=True, tag=f"x2_{i}")
            o_yl = P.add("Q", lambda s, i=i, par=par: s.dma_start(
                out=yfmc[:, par * 1024:(par + 1) * 1024],
                in_=bass.AP(yfm_d.tensor, yfm_d.offset + i * 1024, [[T * 8, 128], [1, 1024]])),
                deps=[o_ysp[i]] + ([o_ysp[NC - 1]] if i == 0 else []) +
                     (prev_b4.get(("omm", i - 2), [])),
                dma=True, tag=f"yl{i}")
            o_omm = []
            for nt in range(2):
                bank = pb[2 * par + nt]
                for kf in range(8):
                    dep = ([o_yl] + o_wol +
                           ([prev_b4.get(("hres", i - 2, nt))] if i >= 2 else [])) if kf == 0 else []
                    P.add("T", lambda t, kf=kf, nt=nt, bank=bank, par=par: t.matmul(
                        bank[:, :], yfmc[:, par * 1024 + kf * 128:par * 1024 + kf * 128 + 128],
                        dslice(w_oT, kf * F + nt * 512, kf * F + nt * 512 + 512),
                        start=(kf == 0), stop=False), deps=dep, tag=f"omm{i}_{nt}k{kf}")
                last = P.add("T", lambda t, nt=nt, bank=bank: t.matmul(
                    bank[:, :], ones_b[:, 0:128],
                    rawap(rows_b, 0, 256 + nt * 512, [[4096, 1], [1, 512]]),
                    start=False, stop=True), deps=[], tag=f"obias{i}_{nt}")
                o_omm.append(last)
            prev_b4[("omm", i)] = o_omm
            o_l0 = P.add("V", lambda v, i=i, par=par, s_ln0=s_ln0: v.scalar_tensor_tensor(
                out=rawap(stg, 0, s_ln0, [["P", 128], [1, F]]),
                in0=x_t[:, par * F:(par + 1) * F], scalar=mu_s[:, i:i + 1],
                in1=bass.AP(rs_s[:].tensor, rs_s[:].offset + i, [[NTT, 128], [0, F]]),
                op0=ALU.subtract, op1=ALU.mult),
                deps=[o_x2] + ([prev_b4.get(("g1", i - 2))] if i >= 2 else []), tag=f"l0r{i}")
            o_hr = []
            for nt in range(2):
                o_hr.append(P.add("V", lambda v, nt=nt, par=par, s_ln0=s_ln0, s_hr=s_hr: v.tensor_tensor(
                    out=rawap(stg, 0, s_hr + nt * 512, [["P", 128], [1, 512]]),
                    in0=rawap(stg, 0, s_ln0 + nt * 512, [["P", 128], [1, 512]]),
                    in1=pb[2 * par + nt][:, :], op=ALU.add),
                    deps=[o_l0, o_omm[nt]] + ([prev_b4.get(("xn", i - 2))] if i >= 2 else []),
                    tag=f"hres{i}_{nt}"))
                prev_b4[("hres", i, nt)] = o_hr[-1]
            hres = rawap(stg, 0, s_hr, [["P", 128], [1, F]])
            stat_war = ([prev_b4.get(("nrm", i - 2)), prev_b4.get(("rs2", i - 2))]
                        if i >= 2 else [])
            o_s1 = P.add("S", lambda sc, hres=hres, sb=sb: sc.activation(
                out=lntm[:, 0:F], in_=hres, func=AF.Copy,
                accum_out=rawap(st4, 0, sb + 0, [["P", 128], [1, 1]])),
                deps=o_hr + stat_war + [dumg[0]], tag=f"h1s{i}")
            dumg[0] = o_s1
            o_s2 = P.add("S", lambda sc, hres=hres, sb=sb: sc.activation(
                out=lntm[:, 0:F], in_=hres, func=AF.Square,
                accum_out=rawap(st4, 0, sb + 1, [["P", 128], [1, 1]])),
                deps=o_hr + stat_war + [dumg[0]], tag=f"h2s{i}")
            dumg[0] = o_s2
            o_v1 = P.add("V", lambda v, sb=sb: v.tensor_scalar(
                out=rawap(st4, 0, sb + 2, [["P", 128], [1, 1]]), in0=rawap(st4, 0, sb + 0, [["P", 128], [1, 1]]),
                scalar1=1.0 / F, scalar2=0.0, op0=ALU.mult, op1=ALU.add), deps=[o_s1], tag=f"hmu{i}")
            o_v2 = P.add("V", lambda v, sb=sb: v.tensor_scalar(
                out=rawap(st4, 0, sb + 3, [["P", 128], [1, 1]]), in0=rawap(st4, 0, sb + 1, [["P", 128], [1, 1]]),
                scalar1=1.0 / F, scalar2=EPS, op0=ALU.mult, op1=ALU.add), deps=[o_s2], tag=f"hs2{i}")
            o_v3 = P.add("V", lambda v, sb=sb: v.tensor_tensor(
                out=rawap(st4, 0, sb + 4, [["P", 128], [1, 1]]), in0=rawap(st4, 0, sb + 2, [["P", 128], [1, 1]]),
                in1=rawap(st4, 0, sb + 2, [["P", 128], [1, 1]]), op=ALU.mult), deps=[o_v1], tag=f"hm2{i}")
            o_v4 = P.add("V", lambda v, sb=sb: v.tensor_tensor(
                out=rawap(st4, 0, sb + 5, [["P", 128], [1, 1]]), in0=rawap(st4, 0, sb + 3, [["P", 128], [1, 1]]),
                in1=rawap(st4, 0, sb + 4, [["P", 128], [1, 1]]), op=ALU.subtract), deps=[o_v2, o_v3], tag=f"hv{i}")
            o_v5 = P.add("V", lambda v, sb=sb: v.reciprocal(
                out=rawap(st4, 0, sb + 6, [["P", 128], [1, 1]]), in_=rawap(st4, 0, sb + 5, [["P", 128], [1, 1]])),
                deps=[o_v4], tag=f"hr{i}")
            o_rsx = P.add("S", lambda sc, sb=sb: sc.activation(
                out=rawap(st4, 0, sb + 7, [["P", 128], [1, 1]]), in_=rawap(st4, 0, sb + 6, [["P", 128], [1, 1]]),
                func=AF.Sqrt), deps=[o_v5], tag=f"hrs{i}")
            o_nrm = P.add("V", lambda v, hres=hres, sb=sb, s_ln0=s_ln0: v.scalar_tensor_tensor(
                out=rawap(stg, 0, s_ln0, [["P", 128], [1, F]]), in0=hres,
                scalar=rawap(st4, 0, sb + 2, [["P", 128], [1, 1]]),
                in1=rawap(st4, 0, sb + 7, [["P", 128], [0, F]]),
                op0=ALU.subtract, op1=ALU.mult), deps=[o_rsx] + o_hr, tag=f"nrm{i}")
            prev_b4[("nrm", i)] = o_nrm
            o_g1 = P.add("G", lambda g, s_ln0=s_ln0, s_hr=s_hr: g.tensor_tensor(
                out=rawap(stg, 0, s_hr, [["P", 128], [1, F]]),
                in0=rawap(stg, 0, s_ln0, [["P", 128], [1, F]]), in1=lng_r.sl(0, 1024),
                op=ALU.mult), deps=[o_nrm, o_s1, o_s2, o_lng], tag=f"g1_{i}")
            prev_b4[("g1", i)] = o_g1
            o_g2 = P.add("G", lambda g, s_hr=s_hr: g.tensor_tensor(
                out=rawap(stg, 0, s_hr, [["P", 128], [1, F]]),
                in0=rawap(stg, 0, s_hr, [["P", 128], [1, F]]), in1=lnb_r.sl(0, 1024),
                op=ALU.add), deps=[o_g1, o_lnb], tag=f"g2_{i}")
            o_g3 = P.add("G", lambda g, par=par, s_hr=s_hr: g.tensor_tensor(
                out=rawap(stg, 0, s_hr, [["P", 128], [1, F]]),
                in0=rawap(stg, 0, s_hr, [["P", 128], [1, F]]),
                in1=x_t[:, par * F:(par + 1) * F], op=ALU.add), deps=[o_g2], tag=f"g3_{i}")
            xnew = rawap(stg, 0, s_hr, [["P", 128], [1, F]])
            o_t1 = P.add("S", lambda sc, xnew=xnew, sb=sb: sc.activation(
                out=lntm[:, 0:F], in_=xnew, func=AF.Copy,
                accum_out=rawap(st4, 0, sb + 8, [["P", 128], [1, 1]])), deps=[o_g3, dumg[0]], tag=f"t1_{i}")
            dumg[0] = o_t1
            o_t2 = P.add("S", lambda sc, xnew=xnew, sb=sb: sc.activation(
                out=lntm[:, 0:F], in_=xnew, func=AF.Square,
                accum_out=rawap(st4, 0, sb + 9, [["P", 128], [1, 1]])), deps=[o_g3, dumg[0]], tag=f"t2_{i}")
            dumg[0] = o_t2
            o_w1 = P.add("V", lambda v, i=i, sb=sb: v.tensor_scalar(
                out=mu2_s[:, i:i + 1], in0=rawap(st4, 0, sb + 8, [["P", 128], [1, 1]]),
                scalar1=1.0 / F, scalar2=0.0, op0=ALU.mult, op1=ALU.add), deps=[o_t1], tag=f"w1_{i}")
            o_w2 = P.add("V", lambda v, sb=sb: v.tensor_scalar(
                out=rawap(st4, 0, sb + 10, [["P", 128], [1, 1]]), in0=rawap(st4, 0, sb + 9, [["P", 128], [1, 1]]),
                scalar1=1.0 / F, scalar2=EPS, op0=ALU.mult, op1=ALU.add), deps=[o_t2], tag=f"w2_{i}")
            o_w3 = P.add("V", lambda v, i=i, sb=sb: v.tensor_tensor(
                out=rawap(st4, 0, sb + 11, [["P", 128], [1, 1]]), in0=mu2_s[:, i:i + 1],
                in1=mu2_s[:, i:i + 1], op=ALU.mult), deps=[o_w1], tag=f"w3_{i}")
            o_w4 = P.add("V", lambda v, sb=sb: v.tensor_tensor(
                out=rawap(st4, 0, sb + 12, [["P", 128], [1, 1]]), in0=rawap(st4, 0, sb + 10, [["P", 128], [1, 1]]),
                in1=rawap(st4, 0, sb + 11, [["P", 128], [1, 1]]), op=ALU.subtract), deps=[o_w2, o_w3], tag=f"w4_{i}")
            o_w5 = P.add("V", lambda v, sb=sb: v.reciprocal(
                out=rawap(st4, 0, sb + 13, [["P", 128], [1, 1]]), in_=rawap(st4, 0, sb + 12, [["P", 128], [1, 1]])),
                deps=[o_w4], tag=f"w5_{i}")
            o_rs2 = P.add("S", lambda sc, i=i, sb=sb: sc.activation(
                out=rs2_s[:, i:i + 1], in_=rawap(st4, 0, sb + 13, [["P", 128], [1, 1]]),
                func=AF.Sqrt), deps=[o_w5], tag=f"rs2_{i}")
            prev_b4[("rs2", i)] = o_rs2
            o_xn[i] = P.add("Q", lambda s, i=i, xnew=xnew: s.dma_start(
                out=xnew_d[128 * i:128 * (i + 1), :], in_=xnew),
                deps=[o_t1, o_t2, o_rs2], dma=True, tag=f"xnsp{i}")
            prev_b4[("xn", i)] = o_xn[i]
            fq_drip(10)
        fq_drip(10000)
        P.emit_block()
        if max_phase < 5:
            return nc

        NM5b = T // 512
        o_out_all = []
        ring_guard_fc = {}
        ring_guard_pj = {}
        tr_by_tile = {}
        ln2_by_tile = {}
        prev_gel = {}
        prev_l2e = None
        fo_by_bank = {}
        xr_guard = {0: [], 1: []}
        o_xr = {}
        for mq in range(NM5b):
            b2f = (mq % 2) * 4096
            o_tr2 = []
            for ii in range(4):
                i = 4 * mq + ii
                dep = [o_wq_last["pr"], o_xn[i]] + \
                      ([ln2_by_tile[i - 2]] if (i - 2) in ln2_by_tile else [])
                o_xl = P.add("Q", lambda s, i=i, ii=ii: s.dma_start(
                    out=x_t[:, (ii % 2) * F:(ii % 2 + 1) * F],
                    in_=xnew_d[128 * i:128 * (i + 1), :]), deps=dep, dma=True, tag=f"xnl{i}")
                o_l2 = P.add("V", lambda v, i=i, ii=ii: v.scalar_tensor_tensor(
                    out=lntm[:, (ii % 2) * F:(ii % 2 + 1) * F],
                    in0=x_t[:, (ii % 2) * F:(ii % 2 + 1) * F], scalar=mu2_s[:, i:i + 1],
                    in1=bass.AP(rs2_s[:].tensor, rs2_s[:].offset + i, [[NTT, 128], [0, F]]),
                    op0=ALU.subtract, op1=ALU.mult),
                    deps=[o_xl] + tr_by_tile.get(i - 2, []), tag=f"ln2_{i}")
                ln2_by_tile[i] = o_l2
                war = [prev_gel[mq - 2]] if (mq - 2) in prev_gel else []
                tilet = []
                for fb in range(8):
                    tilet.append(add_tr(lambda e, ii=ii, fb=fb, b2f=b2f: e.dma_start(
                        out=dslice(ln2f, b2f + fb * 512 + ii * 128,
                                   b2f + fb * 512 + ii * 128 + 128),
                        in_=lntm[:, (ii % 2) * F + fb * 128:(ii % 2) * F + fb * 128 + 128],
                        transpose=True), deps=[o_l2] + war, tag=f"tr2_{i}_{fb}", eng="S"))
                o_tr2 += tilet
                tr_by_tile[i] = tilet
            o_lmm = None
            for kf in range(8):
                dep = (o_tr2 + wload + ([prev_l2e] if prev_l2e else [])) if kf == 0 else []
                o_lmm = P.add("T", lambda t, kf=kf, b2f=b2f: t.matmul(
                    pb[4][0:16, 0:512], lafc[:, kf * 16:(kf + 1) * 16],
                    dslice(ln2f, b2f + kf * 512, b2f + (kf + 1) * 512),
                    start=(kf == 0), stop=(kf == 7)), deps=dep, tag=f"l1mm{mq}k{kf}")
            o_l1e = P.add("S", lambda sc: sc.mul(
                out=rawap(lora_b, 0, 0, [["P", 16], [1, 512]]), in_=pb[4][0:16, 0:512], mul=2.0),
                deps=[o_lmm], tag=f"l1ev{mq}")
            o_gel = []
            for ft in range(32):
                slot = ft % 8
                o_ws = P.add("Q", lambda s, ft=ft, slot=slot: s.dma_start(
                    out=dslice(fcW, slot * 1024, (slot + 1) * 1024),
                    in_=bass.AP(wqf_d.tensor, wqf_d.offset + ft * 128,
                                [[8 * FF, 128], [FF, 8], [1, 128]])),
                    deps=[o_wq_last["fc"], o_wq_last["pr"]] +
                         ([ring_guard_fc.get(slot)] if ring_guard_fc.get(slot) else []),
                    dma=True, tag=f"fws{mq}f{ft}")
                bank = pb[ft % 2]
                prevg = o_gel[ft - 2] if ft >= 2 else fo_by_bank.get(ft % 2)
                last = None
                for kf in range(8):
                    dep = ([o_l1e, o_ws] + o_tr2 + ([prevg] if prevg else [])) if kf == 0 else []
                    last = P.add("T", lambda t, kf=kf, slot=slot, bank=bank, b2f=b2f: t.matmul(
                        bank[:, 0:512], dslice(fcW, slot * 1024 + kf * 128, slot * 1024 + kf * 128 + 128),
                        dslice(ln2f, b2f + kf * 512, b2f + (kf + 1) * 512),
                        start=(kf == 0), stop=False), deps=dep, tag=f"fc{mq}f{ft}k{kf}")
                last = P.add("T", lambda t, ft=ft, bank=bank: t.matmul(
                    bank[:, 0:512], bass.AP(lbfcr[:].tensor, lbfcr[:].offset + ft * 128,
                                            [[FF, 1], [1, 128]]),
                    ones_b[:, 0:512], start=False, stop=False), deps=[], tag=f"fcb{mq}f{ft}")
                last = P.add("T", lambda t, ft=ft, bank=bank: t.matmul(
                    bank[:, 0:512], lbfc[:, ft * 128:ft * 128 + 128],
                    rawap(lora_b, 0, 0, [["P", 16], [1, 512]]), start=False, stop=True),
                    deps=[], tag=f"fcl{mq}f{ft}")
                ring_guard_fc[slot] = last
                o_gel.append(P.add("S", lambda sc, ft=ft, bank=bank: sc.activation(
                    out=dslice(h1b, ft * 512, (ft + 1) * 512), in_=bank[:, 0:512],
                    func=AF.Gelu), deps=[last], tag=f"gel{mq}f{ft}"))
            prev_gel[mq] = o_gel[-1]
            o_l2mm = None
            for ft in range(32):
                o_l2mm = P.add("T", lambda t, ft=ft: t.matmul(
                    pb[4][0:16, 0:512], lapr[:, ft * 16:(ft + 1) * 16],
                    dslice(h1b, ft * 512, (ft + 1) * 512),
                    start=(ft == 0), stop=(ft == 31)),
                    deps=[o_gel[ft]] + ([o_l1e] if ft == 0 else []), tag=f"l2mm{mq}f{ft}")
            o_l2e = P.add("S", lambda sc: sc.mul(
                out=rawap(lora_b, 0, 512, [["P", 16], [1, 512]]), in_=pb[4][0:16, 0:512], mul=2.0),
                deps=[o_l2mm], tag=f"l2ev{mq}")
            prev_l2e = o_l2e
            for ii in range(2):
                i = 4 * mq + ii
                o_xr[ii] = P.add("Q", lambda s, i=i, ii=ii: s.dma_start(
                    out=rawap(Pst, 0, 2048 + (ii % 2) * 1024, [["P", 128], [1, F]]),
                    in_=xnew_d[128 * i:128 * (i + 1), :]),
                    deps=[o_xn[i]] + xr_guard[ii % 2], dma=True, tag=f"xr{i}")
            lastp = {}
            for ft in range(32):
                slot = ft % 8
                o_ws = P.add("G", lambda s, ft=ft, slot=slot: s.dma_start(
                    out=dslice(pjW, slot * 1024, (slot + 1) * 1024),
                    in_=bass.AP(wqp_d.tensor, wqp_d.offset + ft * F,
                                [[32 * F, 128], [1, F]])),
                    deps=[o_wq_last["pr"]] + ([ring_guard_pj.get(slot)] if ring_guard_pj.get(slot) else []),
                    dma=True, tag=f"pws{mq}f{ft}")
                for ii in range(4):
                    for nt in range(2):
                        bank = pb[2 * ii + nt]
                        if ft == 0:
                            dep = [o_gel[-1], o_l2e, o_ws] + (o_out_all[-8:] if o_out_all else [])
                        else:
                            dep = [o_ws]
                        lastp[(ii, nt)] = P.add("T", lambda t, ft=ft, ii=ii, nt=nt, bank=bank, slot=slot:
                                                t.matmul(bank[:, :],
                                                         dslice(h1b, ft * 512 + ii * 128, ft * 512 + ii * 128 + 128),
                                                         dslice(pjW, slot * 1024 + nt * 512, slot * 1024 + nt * 512 + 512),
                                                         start=(ft == 0), stop=False),
                                                deps=dep, tag=f"pj{mq}i{ii}n{nt}f{ft}")
                        ring_guard_pj[slot] = lastp[(ii, nt)]
            for ii in range(4):
                i = 4 * mq + ii
                fo_pair = []
                for nt in range(2):
                    bank = pb[2 * ii + nt]
                    l1 = P.add("T", lambda t, nt=nt, bank=bank: t.matmul(
                        bank[:, :], ones_b[:, 0:128],
                        rawap(rows_b, 0, 1280 + nt * 512, [[4096, 1], [1, 512]]),
                        start=False, stop=False),
                        deps=[lastp[(ii, nt)]], tag=f"pjb{i}n{nt}")
                    l2 = P.add("T", lambda t, nt=nt, bank=bank, ii=ii: t.matmul(
                        bank[:, :], rawap(lora_b, 0, 512 + ii * 128, [["P", 16], [1, 128]]),
                        lbpr[:, nt * 512:nt * 512 + 512], start=False, stop=True),
                        deps=[], tag=f"pjl{i}n{nt}")
                    o_fo = P.add("V", lambda v, ii=ii, nt=nt, bank=bank: v.tensor_tensor(
                        out=rawap(stg, 0, 1024 + nt * 512, [["P", 128], [1, 512]]),
                        in0=rawap(Pst, 0, 2048 + (ii % 2) * 1024 + nt * 512, [["P", 128], [1, 512]]),
                        in1=bank[:, :], op=ALU.add),
                        deps=[l2, o_xr[ii]] + (o_out_all[-2:] if o_out_all else []), tag=f"fo{i}n{nt}")
                    fo_pair.append(o_fo)
                    fo_by_bank[2 * ii + nt] = o_fo
                    o_out_all.append(P.add("Q", lambda s, i=i, nt=nt: s.dma_start(
                        out=out_d[128 * i:128 * (i + 1), nt * 512:nt * 512 + 512],
                        in_=rawap(stg, 0, 1024 + nt * 512, [["P", 128], [1, 512]])),
                        deps=[o_fo], dma=True, tag=f"outd{i}n{nt}"))
                xr_guard[ii % 2] = fo_pair
                if ii + 2 <= 3:
                    i2 = 4 * mq + ii + 2
                    o_xr[ii + 2] = P.add("Q", lambda s, i2=i2, ii=ii: s.dma_start(
                        out=rawap(Pst, 0, 2048 + (ii % 2) * 1024, [["P", 128], [1, F]]),
                        in_=xnew_d[128 * i2:128 * (i2 + 1), :]),
                        deps=[o_xn[i2]] + fo_pair, dma=True, tag=f"xr{i2}")
        P.add("Q", lambda s: s.nop(), deps=o_out_all, tag="final_wait")
        P.emit_block()

    return nc



_NC_CACHE = {}


def kernel(**inputs):
    Bt, Tfull, _ = inputs["L0_attn_x"].shape
    n_cores = 8
    spb = n_cores // Bt
    T = Tfull // spb

    hw = host_weights(inputs)
    ht = host_tables()

    key = (T,)
    if key not in _NC_CACHE:
        _NC_CACHE[key] = build_nc(T, n_ranks=n_cores)
    nc = _NC_CACHE[key]

    NC_ = T // C
    in_maps = []
    for core in range(n_cores):
        b = core // spb
        sl = core % spb
        t0 = sl * T
        cosT, sinT = host_rope(Tfull, t0, T)
        msel = np.zeros((1, 8), np.float32)
        gap = np.zeros((1, 8), np.float32)
        for j in range(n_cores):
            jb, jsl = j // spb, j % spb
            if jb == b and jsl < sl:
                msel[0, j] = 1.0
                gap[0, j] = float(T * (sl - 1 - jsl))
        m = {"x": np.ascontiguousarray(inputs["L0_attn_x"][b, t0:t0 + T]),
             "cosT": cosT, "sinT": sinT, "msel": msel, "gap4096": gap,
             "k128row": (128.0 * np.arange(NC_, dtype=np.float32))[None, :]}
        m.update(hw)
        m.update(ht)
        in_maps.append(m)

    res = run_bass_kernel_spmd(nc, in_maps, core_ids=list(range(n_cores)))
    out = np.empty((Bt, Tfull, F), np.float32)
    for core in range(n_cores):
        b = core // spb
        sl = core % spb
        out[b, sl * T:(sl + 1) * T] = res.results[core]["out"]
    return out


# version 10
# speedup vs baseline: 1.1021x; 1.1021x over previous
import math
import numpy as np
import ml_dtypes

import concourse.bass as bass
import concourse.mybir as mybir
from concourse.bass_utils import run_bass_kernel_spmd

F32 = mybir.dt.float32
BF16 = mybir.dt.bfloat16
AF = mybir.ActivationFunctionType
ALU = mybir.AluOpType
BF = ml_dtypes.bfloat16

H, R, D, F, FF = 16, 16, 64, 1024, 4096
C, B, NG = 128, 16, 2
LMIN = -5.0
TLAM_CAP = 40.0
EPS = 1e-5
RND = float(2 ** 23)
SCL = 0.125
STUB = set()


def pitch(h):
    return h[:].ap[0][0]


class TV:

    def __init__(self, h, base, ncols):
        self.h, self.base, self.ncols = h, base, ncols
        self.pitch = pitch(h)

    def ap(self, part_off, free_off, dims):
        full = self.h[:]
        return bass.AP(full.tensor,
                       full.offset + part_off * self.pitch + self.base + free_off,
                       [[self.pitch if s == "P" else s, n] for s, n in dims])

    def sl(self, a, b, p0=0, p1=128):
        return self.ap(p0, a, [["P", p1 - p0], [1, b - a]])


def rawap(h, part_off, free_off, dims):
    if isinstance(h, TV):
        return h.ap(part_off, free_off, dims)
    full = h[:]
    p = full.ap[0][0]
    return bass.AP(full.tensor, full.offset + part_off * p + free_off,
                   [[p if s == "P" else s, n] for s, n in dims])


def dslice(h, a, b, p0=0, p1=128):
    if isinstance(h, TV):
        return h.sl(a, b, p0, p1)
    return h[p0:p1, a:b]



NSLOT = 8


class Op:
    __slots__ = ("eng", "fn", "deps", "dma", "coll", "need_inc", "sem", "sem_val",
                 "tag", "emitted")

    def __init__(self, eng, fn, deps, dma, tag="", coll=False):
        self.eng, self.fn, self.dma, self.tag = eng, fn, dma, tag
        self.coll = coll
        self.deps = [d for d in deps if d is not None]
        self.need_inc = False
        self.sem = None
        self.sem_val = None
        self.emitted = False


class Prog:

    ENGS = ("T", "V", "S", "G", "Q")
    DMAE = ("S", "G", "Q")

    def __init__(self, nc, stack):
        self.nc = nc
        self.ops = {e: [] for e in self.ENGS}
        self.sems = {e: stack.enter_context(nc.semaphore(f"sem_{e}")) for e in self.ENGS}
        self.dsem = {e: [stack.enter_context(nc.semaphore(f"dma_{e}{i}"))
                         for i in range(NSLOT)] for e in self.DMAE}
        self.csem = stack.enter_context(nc.semaphore("sem_cc"))
        self.tot = {e: 0 for e in self.ENGS}
        self.dcnt = {e: 0 for e in self.DMAE}
        self.ccnt = 0

    def add(self, eng, fn, deps=(), dma=False, tag="", coll=False):
        op = Op(eng, fn, deps, dma, tag, coll)
        self.ops[eng].append(op)
        return op

    def emit_block(self):
        nc = self.nc
        for e in self.ENGS:
            for op in self.ops[e]:
                if op.dma or op.coll:
                    op.need_inc = True
                for d in op.deps:
                    if d.emitted:
                        continue
                    if d.eng == op.eng == "T" and not (d.dma or d.coll):
                        continue
                    d.need_inc = True
        for e in self.ENGS:
            t = self.tot[e]
            for op in self.ops[e]:
                if op.coll:
                    self.ccnt += 1
                    op.sem = self.csem
                    op.sem_val = self.ccnt
                elif op.dma:
                    j = self.dcnt[e]
                    self.dcnt[e] = j + 1
                    op.sem = self.dsem[e][j % NSLOT]
                    op.sem_val = 16 * (j // NSLOT + 1)
                elif op.need_inc:
                    t += 1
                    op.sem = self.sems[e]
                    op.sem_val = t
            self.tot[e] = t
        engmap = dict(T="tensor", V="vector", S="scalar", G="gpsimd", Q="sync")
        with nc.Block() as blk:
            for e in self.ENGS:
                if not self.ops[e]:
                    continue
                ops_e = self.ops[e]

                def body(engine, ops_e=ops_e, e=e):
                    wm = {}
                    for op in ops_e:
                        if op.dma and op.sem_val > 16:
                            key = id(op.sem)
                            pv = op.sem_val - 16
                            if wm.get(key, 0) < pv:
                                engine.wait_ge(op.sem, pv)
                                wm[key] = pv
                        for d in op.deps:
                            if d.emitted and not (d.dma or d.coll):
                                continue
                            if d.eng == op.eng == "T" and not (d.dma or d.coll):
                                continue
                            key = id(d.sem)
                            if wm.get(key, 0) >= d.sem_val:
                                continue
                            engine.wait_ge(d.sem, d.sem_val)
                            wm[key] = d.sem_val
                        inst = op.fn(engine)
                        if op.need_inc:
                            assert inst is not None, op.tag
                            inst.then_inc(op.sem, 16 if op.dma else 1)
                getattr(blk, engmap[e])(body)
        for e in self.ENGS:
            for op in self.ops[e]:
                op.emitted = True
        self.ops = {e: [] for e in self.ENGS}



def host_tables():
    t = {}
    e = np.zeros((1, 8 * C), np.float32)
    for j in range(8):
        for c in range(C):
            e[0, j * C + c] = c - B * j - 8
    t["e_tlam"] = e
    t["e_tplus"] = (np.arange(C, dtype=np.float32) + 1.0)[None, :]
    t["e_tminus"] = (127.0 - np.arange(C, dtype=np.float32))[None, :]
    t["e_tp"] = (8.0 - np.arange(B, dtype=np.float32))[None, :]
    m = np.zeros((128, 8 * C), BF)
    for hl in range(8):
        for sp in range(B):
            for j in range(8):
                for c in range(C):
                    if c >= B * j + sp:
                        m[hl * B + sp, j * C + c] = 1
    t["tri_mask"] = m
    bm = np.zeros((128, 128), BF)
    for p in range(4):
        for e_ in range(2):
            bm[32 * p + 16 * e_:32 * p + 16 * e_ + 16, 64 * e_:64 * e_ + 64] = 1
    t["blk_mask"] = bm
    t["ones_row_b"] = np.ones((1, 512), BF)
    return t


def host_weights(w):
    out = {}
    perm = np.zeros(F, np.int64)
    for par in range(2):
        for h_ in range(H):
            for i in range(D // 2):
                perm[par * 512 + h_ * 32 + i] = h_ * D + 2 * i + par
    out["qT"] = np.ascontiguousarray(w["L0_attn_q_w"][perm].T)
    out["kT"] = np.ascontiguousarray(w["L0_attn_k_w"][perm].T)

    def bd(Wq):
        m = np.zeros((F, H * R), np.float32)
        par_i = np.arange(D) % 2
        i_i = np.arange(D) // 2
        for h_ in range(H):
            rows = par_i * 512 + h_ * 32 + i_i
            m[rows, R * h_:R * h_ + R] = Wq[h_].T
        return m
    out["wqbd"] = bd(w["L0_attn_Wq"])
    out["wkbd"] = bd(w["L0_attn_Wk"])
    out["g_wT"] = np.ascontiguousarray(w["L0_attn_g_w"].T)
    out["g_b_row"] = np.ascontiguousarray(w["L0_attn_g_b"][None, :])
    out["v_wT"] = np.ascontiguousarray(w["L0_attn_v_w"].T)
    out["out_wT"] = np.ascontiguousarray(w["L0_attn_out_w"].T)
    out["out_b_row"] = np.ascontiguousarray(w["L0_attn_out_b"][None, :])
    out["ln_g"] = np.ascontiguousarray(w["L0_attn_ln_g"][None, :])
    out["ln_b"] = np.ascontiguousarray(w["L0_attn_ln_b"][None, :])
    out["wfcT"] = np.ascontiguousarray(w["L0_ffn_W_fc"].T)
    out["b_fc_row"] = np.ascontiguousarray(w["L0_ffn_b_fc"][None, :])
    out["a_fcT"] = np.ascontiguousarray(w["L0_ffn_A_fc"].T)
    out["b_fcBT"] = np.ascontiguousarray(w["L0_ffn_B_fc"].T)
    out["wprojT"] = np.ascontiguousarray(w["L0_ffn_W_proj"].T)
    out["b_proj_row"] = np.ascontiguousarray(w["L0_ffn_b_proj"][None, :])
    out["a_projT"] = np.ascontiguousarray(w["L0_ffn_A_proj"].T)
    out["b_projBT"] = np.ascontiguousarray(w["L0_ffn_B_proj"].T)
    out["decay_logits"] = np.ascontiguousarray(
        w["L0_attn_base_decay_logits"].reshape(1, -1))
    return out


_ROPE_CACHE = {}


def host_rope(Tfull, t0, T):
    key = (Tfull, t0, T)
    if key in _ROPE_CACHE:
        return _ROPE_CACHE[key]
    import jax
    import jax.numpy as jnp
    with jax.default_device(jax.devices("cpu")[0]):
        half = D // 2
        inv_freq = 1.0 / 10000.0 ** (jnp.arange(half, dtype=jnp.float32) / half)
        freqs = jnp.arange(Tfull, dtype=jnp.float32)[:, None] * inv_freq[None, :]
        cos = np.asarray(jnp.cos(freqs), np.float32)[t0:t0 + T]
        sin = np.asarray(jnp.sin(freqs), np.float32)[t0:t0 + T]
    cosT = np.ascontiguousarray(np.tile(cos.T, (4, 1))).astype(BF)
    sinT = np.ascontiguousarray(np.tile(sin.T, (4, 1))).astype(BF)
    _ROPE_CACHE[key] = (cosT, sinT)
    return cosT, sinT



def build_nc(T, n_ranks=8, max_phase=9, dbg_spills=False):
    NC = T // C
    NM = T // 512
    NTT = T // 128
    NM5 = T // 256
    nc = bass.Bass("TRN2", target_bir_lowering=False)

    def din(name, shape, dt=F32):
        return nc.dram_tensor(name, list(shape), dt, kind="ExternalInput").ap()

    x_in = din("x", [T, F])
    qT_d = din("qT", [F, F]); kT_d = din("kT", [F, F])
    wqbd_d = din("wqbd", [F, 256]); wkbd_d = din("wkbd", [F, 256])
    gwT_d = din("g_wT", [F, 256]); gb_d = din("g_b_row", [1, 256])
    vwT_d = din("v_wT", [F, F])
    owT_d = din("out_wT", [F, F]); ob_d = din("out_b_row", [1, F])
    lng_d = din("ln_g", [1, F]); lnb_d = din("ln_b", [1, F])
    wfcT_d = din("wfcT", [F, FF]); bfc_d = din("b_fc_row", [1, FF])
    afcT_d = din("a_fcT", [F, 16]); bfcBT_d = din("b_fcBT", [16, FF])
    wprT_d = din("wprojT", [FF, F]); bpr_d = din("b_proj_row", [1, F])
    aprT_d = din("a_projT", [FF, 16]); bprBT_d = din("b_projBT", [16, F])
    logit_d = din("decay_logits", [1, 256])
    cos_d = din("cosT", [128, T], BF16); sin_d = din("sinT", [128, T], BF16)
    etlam_d = din("e_tlam", [1, 8 * C]); etp_d = din("e_tplus", [1, C])
    etm_d = din("e_tminus", [1, C]); etpp_d = din("e_tp", [1, B])
    k128_d = din("k128row", [1, NC])
    trim_d = din("tri_mask", [128, 8 * C], BF16)
    blkm_d = din("blk_mask", [128, 128], BF16)
    onesb_d = din("ones_row_b", [1, 512], BF16)
    msel_d = din("msel", [1, 8])
    gap_d = din("gap4096", [1, 8])

    out_d = nc.dram_tensor("out", [T, F], F32, kind="ExternalOutput").ap()

    _k = dict(kind="ExternalOutput") if dbg_spills else {}
    if dbg_spills:
        ln2f_d = nc.dram_tensor("ln2f_dbg", [128, (T // 256) * 2048], BF16, **_k).ap()
        h1_d = nc.dram_tensor("h1_dbg", [128, (T // 256) * 8192], BF16, **_k).ap()
    else:
        ln2f_d = h1_d = None
    yfm_d = nc.dram_tensor("yfm_spill", [128, T * 8], BF16, **_k).ap()
    xnew_d = nc.dram_tensor("xnew_spill", [T, F], F32, **_k).ap()
    v_d = nc.dram_tensor("v_spill", [T, F], BF16, **_k).ap()
    at_d = nc.dram_tensor("at_spill", [128, NG * T], BF16, **_k).ap()
    qr_d = nc.dram_tensor("qr_spill", [128, NG * T], BF16, **_k).ap()
    wqf_d = nc.dram_tensor("wqf_spill", [128, 8 * FF], BF16, **_k).ap()
    wqp_d = nc.dram_tensor("wqp_spill", [128, 32 * F], BF16, **_k).ap()
    cc_in = nc.dram_tensor("cc_in", [NG * 128, 128], F32)
    cc_out = nc.dram_tensor("cc_out", [n_ranks * NG * 128, 128], F32)
    srow_d = nc.dram_tensor("srow_d", [1, 8], F32)

    import contextlib
    with contextlib.ExitStack() as stack:
        def sbuf(name, p, f_, dt):
            return stack.enter_context(nc.sbuf_tensor(name, [p, f_], dt))

        P = Prog(nc, stack)

        _last_tr = [None]

        def add_tr(fn, deps=(), tag=""):
            op = P.add("S", fn, deps=list(deps) + ([_last_tr[0]] if _last_tr[0] else []),
                       dma=True, tag=tag)
            _last_tr[0] = op
            return op

        Pst = sbuf("Pst", 128, max(NG * (NC + 1) * 128, 8192), F32)
        stg = sbuf("stg", 128, 4096, F32)
        x_t = sbuf("x_t", 128, 2 * F, F32)
        mu_s = sbuf("mu_s", 128, NTT, F32)
        rs_s = sbuf("rs_s", 128, NTT, F32)
        mu2_s = sbuf("mu2_s", 128, NTT, F32)
        rs2_s = sbuf("rs2_s", 128, NTT, F32)
        st4 = sbuf("st4", 128, 32, F32)
        scr = sbuf("scr", 128, 256, F32)
        lam_t = sbuf("lam_t", 128, NG, F32)
        L_t = sbuf("L_t", 128, NG, F32)
        Lc_t = sbuf("Lc_t", 128, NG, F32)
        lamC_t = sbuf("lamC_t", 128, NG, F32)
        tk_t = sbuf("tk_t", 128, NG * NC, F32)
        wj_t = sbuf("wj_t", 128, NG * 8, F32)
        Msta = sbuf("Msta", 128, NG * 256, F32)
        rows_b = sbuf("rows_b", 1, 4096, BF16)
        ones_b = sbuf("ones_b", 1, 512, BF16)
        lntm = sbuf("lntm", 128, 2 * F, BF16)
        yfmc = sbuf("yfmc", 128, 2 * 8 * 128, BF16)
        tlam = sbuf("tlam", 128, NG * 8 * C, BF16)
        tplus = sbuf("tplus", 128, NG * C, BF16)
        tminfm = sbuf("tminfm", 128, NG * C, BF16)
        tp_t = sbuf("tp_t", 128, NG * B, BF16)
        tri_m = sbuf("tri_m", 128, 8 * C, BF16)
        blk_m = sbuf("blk_m", 128, 128, BF16)
        lafc = sbuf("lafc", 128, 8 * 16, BF16)
        lapr = sbuf("lapr", 128, 32 * 16, BF16)
        lbpr = sbuf("lbpr", 16, F, BF16)
        lora_b = sbuf("lora_b", 16, 1024, BF16)
        lbfc = sbuf("lbfc", 16, FF, BF16)
        lbfcr = sbuf("lbfcr", 1, FF, BF16)
        ki32 = sbuf("ki32", 128, 1024, mybir.dt.int32)

        ARN = sbuf("ARN", 128, 48128, BF16)

        def tv(base, ncols):
            return TV(ARN, base, ncols)

        w_vT = tv(0, 8192)
        w_gT = tv(8192, 2048)
        w_qbd = tv(10240, 2048)
        w_kbd = tv(12288, 2048)
        wqk_s = tv(14336, 4096)
        ln0f = tv(18432, 4096)
        qrot = tv(22528, 4096)
        krot = tv(26624, 4096)
        rtmp = tv(30720, 2048)
        g_fm = tv(32768, 1024)
        kr_fm = tv(33792, 1024)
        a_fm = tv(34816, 1024)
        amf = tv(35840, 1024)
        vck = tv(36864, 4096)
        at_mb = tv(40960, 2048)
        am_mb = tv(43008, 2048)
        qr_mb = tv(45056, 2048)
        cs_s = tv(47104, 1024)
        qt_b = tv(0, 2048)
        ssb = tv(2048, 2048)
        qp_b = tv(4096, 256)
        mkb = tv(4352, 256)
        qrc = tv(4608, 512)
        abd = tv(5120, 4096)
        vbd = tv(9216, 16384)
        mkbd = tv(25600, 8192)
        w_oT = tv(0, 8192)
        lng_r = TV(Pst, 0, 1024)
        lnb_r = TV(Pst, 1024, 1024)
        fcW = tv(0, 8192)
        pjW = tv(8192, 8192)
        h1b = tv(16384, 8192)
        ln2f = tv(24576, 2048)

        ccg = TV(stg, 2048, n_ranks * NG * 128)

        pb = [stack.enter_context(nc.psum_tensor(f"pb{i}", [128, 512], F32)) for i in range(8)]

        wload = []
        for dst, src, ncol in [(w_vT, vwT_d, F), (w_gT, gwT_d, 256),
                               (w_qbd, wqbd_d, 256), (w_kbd, wkbd_d, 256)]:
            for kf in range(8):
                wload.append(P.add("G", lambda g, dst=dst, src=src, ncol=ncol, kf=kf:
                                   g.dma_start(out=dslice(dst, ncol * kf, ncol * (kf + 1)),
                                               in_=src[128 * kf:128 * (kf + 1), :]),
                                   dma=True, tag="wload"))
        wload.append(P.add("G", lambda g: g.dma_start(out=rows_b[:, 0:256], in_=gb_d[:, :]), dma=True, tag="ld"))
        wload.append(P.add("G", lambda g: g.dma_start(out=rows_b[:, 256:1280], in_=ob_d[:, :]), dma=True, tag="ld"))
        wload.append(P.add("G", lambda g: g.dma_start(out=rows_b[:, 1280:2304], in_=bpr_d[:, :]), dma=True, tag="ld"))
        wload.append(P.add("G", lambda g: g.dma_start(out=ones_b[:, :], in_=onesb_d[:, :]), dma=True, tag="ld"))
        for kf in range(8):
            wload.append(P.add("G", lambda g, kf=kf: g.dma_start(
                out=lafc[:, kf * 16:(kf + 1) * 16], in_=afcT_d[128 * kf:128 * (kf + 1), :]),
                dma=True, tag="ld"))
        for kf in range(32):
            wload.append(P.add("G", lambda g, kf=kf: g.dma_start(
                out=lapr[:, kf * 16:(kf + 1) * 16], in_=aprT_d[128 * kf:128 * (kf + 1), :]),
                dma=True, tag="ld"))
        wload.append(P.add("G", lambda g: g.dma_start(out=lbfc[:, :], in_=bfcBT_d[:, :]), dma=True, tag="ld"))
        wload.append(P.add("G", lambda g: g.dma_start(out=lbpr[:, :], in_=bprBT_d[:, :]), dma=True, tag="ld"))
        wload.append(P.add("G", lambda g: g.dma_start(out=lbfcr[:, :], in_=bfc_d[:, :]), dma=True, tag="ld"))

        tload = []
        tload.append(P.add("Q", lambda s: s.dma_start(out=tri_m[:], in_=trim_d[:, :]), dma=True, tag="ld"))
        tload.append(P.add("Q", lambda s: s.dma_start(out=blk_m[:], in_=blkm_d[:, :]), dma=True, tag="ld"))
        lamload = []
        for G in range(NG):
            def f(s, G=G):
                srcg = bass.AP(logit_d.tensor, logit_d.offset + 128 * G, [[1, 128], [1, 1]])
                dstg = bass.AP(lam_t[:].tensor, lam_t[:].offset + G, [[NG, 128], [1, 1]])
                return s.dma_start(out=dstg, in_=srcg)
            o = P.add("Q", f, dma=True, tag=f"lgld{G}")
            tload.append(o); lamload.append(o)

        def bcast_load(dst_h, src_ap, n, coloff=0, eng="Q"):
            def f(s):
                sap = bass.AP(src_ap.tensor, src_ap.offset, [[0, 128], [1, n]])
                dap = rawap(dst_h, 0, coloff, [["P", 128], [1, n]])
                return s.dma_start(out=dap, in_=sap)
            return P.add(eng, f, dma=True, tag="bcast")

        tload.append(bcast_load(scr, msel_d, 8, 0))
        tload.append(bcast_load(scr, gap_d, 8, 8))
        tload.append(bcast_load(scr, k128_d, NC, 16))
        tload.append(bcast_load(stg, etlam_d, 8 * C, 0))
        tload.append(bcast_load(stg, etp_d, C, 1024))
        tload.append(bcast_load(stg, etm_d, C, 1152))
        tload.append(bcast_load(stg, etpp_d, B, 1280))

        o_sig = P.add("S", lambda sc: sc.activation(out=lam_t[:], in_=lam_t[:], func=AF.Sigmoid),
                      deps=lamload, tag="sigmoid")
        o_ln = P.add("S", lambda sc: sc.activation(out=L_t[:], in_=lam_t[:], func=AF.Ln),
                     deps=[o_sig], tag="lnlam")
        o_lc = P.add("V", lambda v: v.tensor_scalar(out=Lc_t[:], in0=L_t[:], scalar1=LMIN,
                                                    scalar2=0.0, op0=ALU.max, op1=ALU.add),
                     deps=[o_ln], tag="Lclamp")

        stage = 1536
        prev_exp = []
        all_exp = []
        for G in range(NG):
            Lg = L_t[:, G:G + 1]
            Lcg = Lc_t[:, G:G + 1]
            argdeps = [o_lc] + tload + prev_exp
            a1 = P.add("V", lambda v, Lcg=Lcg: v.tensor_scalar(
                out=rawap(stg, 0, stage, [["P", 128], [1, 8 * C]]),
                in0=rawap(stg, 0, 0, [["P", 128], [1, 8 * C]]),
                scalar1=Lcg, scalar2=TLAM_CAP, op0=ALU.mult, op1=ALU.min),
                deps=argdeps, tag=f"tlam_arg{G}")
            a2 = P.add("V", lambda v, Lg=Lg: v.tensor_scalar(
                out=rawap(stg, 0, stage + 1024, [["P", 128], [1, C]]),
                in0=rawap(stg, 0, 1024, [["P", 128], [1, C]]),
                scalar1=Lg, scalar2=0.0, op0=ALU.mult, op1=ALU.add),
                deps=argdeps, tag=f"tplus_arg{G}")
            a3 = P.add("V", lambda v, Lg=Lg: v.tensor_scalar(
                out=rawap(stg, 0, stage + 1152, [["P", 128], [1, C]]),
                in0=rawap(stg, 0, 1152, [["P", 128], [1, C]]),
                scalar1=Lg, scalar2=math.log(SCL), op0=ALU.mult, op1=ALU.add),
                deps=argdeps, tag=f"tminus_arg{G}")
            a4 = P.add("V", lambda v, Lcg=Lcg: v.tensor_scalar(
                out=rawap(stg, 0, stage + 1280, [["P", 128], [1, B]]),
                in0=rawap(stg, 0, 1280, [["P", 128], [1, B]]),
                scalar1=Lcg, scalar2=math.log(SCL), op0=ALU.mult, op1=ALU.add),
                deps=argdeps, tag=f"tp_arg{G}")
            a5 = P.add("V", lambda v, Lg=Lg: v.tensor_scalar(
                out=rawap(stg, 0, stage + 1296, [["P", 128], [1, 1]]),
                in0=Lg, scalar1=float(C), scalar2=0.0, op0=ALU.mult, op1=ALU.add),
                deps=argdeps, tag=f"lamC_arg{G}")
            a6 = P.add("V", lambda v, Lg=Lg: v.tensor_tensor(
                out=rawap(stg, 0, stage + 1312, [["P", 128], [1, NC]]),
                in0=bass.AP(Lg.tensor, Lg.offset, [[NG, 128], [0, NC]]),
                in1=rawap(scr, 0, 16, [["P", 128], [1, NC]]), op=ALU.mult),
                deps=argdeps, tag=f"tk_arg{G}")
            a7 = P.add("V", lambda v, Lg=Lg: v.tensor_tensor(
                out=rawap(stg, 0, stage + 1312 + NC, [["P", 128], [1, 8]]),
                in0=bass.AP(Lg.tensor, Lg.offset, [[NG, 128], [0, 8]]),
                in1=rawap(scr, 0, 8, [["P", 128], [1, 8]]), op=ALU.mult),
                deps=argdeps, tag=f"wj_arg{G}")
            exp_deps = [a1, a2, a3, a4, a5, a6, a7]
            g_exp = []
            for dst_ap, src_off, n, dname in [
                (tlam[:, G * 8 * C:(G + 1) * 8 * C], stage, 8 * C, "tlam"),
                (tplus[:, G * C:(G + 1) * C], stage + 1024, C, "tplus"),
                (tminfm[:, G * C:(G + 1) * C], stage + 1152, C, "tminus"),
                (tp_t[:, G * B:(G + 1) * B], stage + 1280, B, "tp"),
                (lamC_t[:, G:G + 1], stage + 1296, 1, "lamC"),
                (tk_t[:, G * NC:(G + 1) * NC], stage + 1312, NC, "tk"),
                (wj_t[:, G * 8:(G + 1) * 8], stage + 1312 + NC, 8, "wj"),
            ]:
                g_exp.append(P.add("S", lambda sc, dst_ap=dst_ap, src_off=src_off, n=n:
                                   sc.activation(out=dst_ap,
                                                 in_=rawap(stg, 0, src_off, [["P", 128], [1, n]]),
                                                 func=AF.Exp),
                                   deps=exp_deps, tag=f"exp_{dname}{G}"))
            prev_exp = g_exp
            all_exp += g_exp
            all_exp.append(P.add("V", lambda v, G=G: v.tensor_tensor(
                out=wj_t[:, G * 8:(G + 1) * 8], in0=wj_t[:, G * 8:(G + 1) * 8],
                in1=rawap(scr, 0, 0, [["P", 128], [1, 8]]), op=ALU.mult),
                deps=g_exp, tag=f"wj_msel{G}"))
        o_pz = P.add("V", lambda v: v.memset(
            rawap(Pst, 0, 0, [["P", 128], [1, NG * (NC + 1) * 128]]), 0.0),
            deps=all_exp, tag="Pzero")
        P.emit_block()

        dumg = [None]
        all_spills = []
        o_xload = [None] * NTT
        o_ln0 = [None] * NTT
        o_tr = [[None] * 8 for _ in range(NTT)]
        o_rs_l = [None] * NTT
        o_rec = [None] * (NG * NC)
        ln0f_last_reader = {}
        bankg = {}
        prev_qr_done = []
        prev_macro_tail = []
        rtmp_guard = [None, None]
        qr_sp_guard = {}
        strip_ctr = [0]
        strip_guard = [None, None, None, None]
        vck_guard = [None] * 4
        atq_guard = {}
        amb_guard = [None, None]

        for m in range(NM):
            csb = 0
            o_cs = []
            csdep = list(atq_guard.get(("cs", m - 1), []))
            for which, src in [(0, cos_d), (1, sin_d)]:
                o_cs.append(P.add("Q", lambda s, which=which, src=src, csb=csb, m=m:
                                  s.dma_start(out=dslice(cs_s, csb + which * 512, csb + which * 512 + 512),
                                              in_=bass.AP(src.tensor, src.offset + m * 512,
                                                          [[T, 128], [1, 512]])),
                                  deps=csdep, dma=True, tag=f"cs{m}"))
            for ii in range(4):
                i = 4 * m + ii
                war = [o_ln0[i - 2]] if i >= 2 else []
                o_xload[i] = P.add("Q", lambda s, i=i: s.dma_start(
                    out=x_t[:, (i % 2) * F:(i % 2 + 1) * F],
                    in_=x_in[128 * i:128 * (i + 1), :]), deps=war, dma=True, tag=f"x{i}")
                xa = x_t[:, (i % 2) * F:(i % 2 + 1) * F]
                sb_ = (i % 2) * 8
                stat_war = [o_rs_l[i - 2]] if i >= 2 else []
                o_s1 = P.add("S", lambda sc, xa=xa, sb_=sb_: sc.activation(
                    out=rawap(stg, 0, 2048, [["P", 128], [1, F]]), in_=xa, func=AF.Copy,
                    accum_out=rawap(st4, 0, sb_ + 0, [["P", 128], [1, 1]])),
                    deps=[o_xload[i], dumg[0]] + stat_war, tag=f"st1_{i}")
                dumg[0] = o_s1
                o_s2 = P.add("S", lambda sc, xa=xa, sb_=sb_: sc.activation(
                    out=rawap(stg, 0, 2048, [["P", 128], [1, F]]), in_=xa, func=AF.Square,
                    accum_out=rawap(st4, 0, sb_ + 1, [["P", 128], [1, 1]])),
                    deps=[o_xload[i], dumg[0]] + stat_war, tag=f"st2_{i}")
                dumg[0] = o_s2
                o_v1 = P.add("V", lambda v, i=i, sb_=sb_: v.tensor_scalar(
                    out=mu_s[:, i:i + 1], in0=rawap(st4, 0, sb_ + 0, [["P", 128], [1, 1]]),
                    scalar1=1.0 / F, scalar2=0.0, op0=ALU.mult, op1=ALU.add),
                    deps=[o_s1], tag=f"mu{i}")
                o_v2 = P.add("V", lambda v, sb_=sb_: v.tensor_scalar(
                    out=rawap(st4, 0, sb_ + 2, [["P", 128], [1, 1]]),
                    in0=rawap(st4, 0, sb_ + 1, [["P", 128], [1, 1]]),
                    scalar1=1.0 / F, scalar2=EPS, op0=ALU.mult, op1=ALU.add),
                    deps=[o_s2], tag=f"s2n{i}")
                o_v3 = P.add("V", lambda v, i=i, sb_=sb_: v.tensor_tensor(
                    out=rawap(st4, 0, sb_ + 3, [["P", 128], [1, 1]]),
                    in0=mu_s[:, i:i + 1], in1=mu_s[:, i:i + 1], op=ALU.mult),
                    deps=[o_v1], tag=f"mu2{i}")
                o_v4 = P.add("V", lambda v, sb_=sb_: v.tensor_tensor(
                    out=rawap(st4, 0, sb_ + 4, [["P", 128], [1, 1]]),
                    in0=rawap(st4, 0, sb_ + 2, [["P", 128], [1, 1]]),
                    in1=rawap(st4, 0, sb_ + 3, [["P", 128], [1, 1]]), op=ALU.subtract),
                    deps=[o_v2, o_v3], tag=f"var{i}")
                o_v5 = P.add("V", lambda v, sb_=sb_: v.reciprocal(
                    out=rawap(st4, 0, sb_ + 5, [["P", 128], [1, 1]]),
                    in_=rawap(st4, 0, sb_ + 4, [["P", 128], [1, 1]])),
                    deps=[o_v4], tag=f"rec{i}")
                o_rs = P.add("S", lambda sc, i=i, sb_=sb_: sc.activation(
                    out=rs_s[:, i:i + 1], in_=rawap(st4, 0, sb_ + 5, [["P", 128], [1, 1]]),
                    func=AF.Sqrt), deps=[o_v5], tag=f"rstd{i}")
                o_rs_l[i] = o_rs
                war2 = o_tr[i - 2] if i >= 2 else []
                o_ln0[i] = P.add("V", lambda v, i=i: v.scalar_tensor_tensor(
                    out=lntm[:, (i % 2) * F:(i % 2 + 1) * F],
                    in0=x_t[:, (i % 2) * F:(i % 2 + 1) * F],
                    scalar=mu_s[:, i:i + 1],
                    in1=bass.AP(rs_s[:].tensor, rs_s[:].offset + i, [[NTT, 128], [0, F]]),
                    op0=ALU.subtract, op1=ALU.mult),
                    deps=[o_rs] + war2, tag=f"ln0_{i}")
                warf = [ln0f_last_reader.get(m - 1)] if m >= 1 else []
                for fb in range(8):
                    o_tr[i][fb] = add_tr(lambda e, i=i, ii=ii, fb=fb: e.dma_start(
                        out=dslice(ln0f, fb * 512 + ii * 128, fb * 512 + ii * 128 + 128),
                        in_=lntm[:, (i % 2) * F + fb * 128:(i % 2) * F + fb * 128 + 128],
                        transpose=True),
                        deps=[o_ln0[i]] + warf, tag=f"tr{i}_{fb}")

            tr_deps = [o_tr[4 * m + ii][fb] for ii in range(4) for fb in range(8)]

            for qk, wsrc, rot in [(0, qT_d, qrot), (1, kT_d, krot)]:
                for mp in range(4):
                    strips = {}
                    for Mt in (mp, mp + 4):
                        slot = strip_ctr[0] % 4
                        strip_ctr[0] += 1
                        sg = strip_guard[slot]
                        o_st = P.add("G", lambda g, wsrc=wsrc, Mt=Mt, slot=slot: g.dma_start(
                            out=dslice(wqk_s, slot * 1024, slot * 1024 + 1024),
                            in_=bass.AP(wsrc.tensor, wsrc.offset + Mt * 128,
                                        [[F, 128], [128 * F, 8], [1, 128]])),
                            deps=[sg], dma=True, tag=f"wst{qk}{Mt}m{m}")
                        strips[Mt] = (slot, o_st)
                    pi = (qk * 4 + mp)
                    be_i, bo_i = pi % 2, 2 + pi % 2
                    bank_e, bank_o = pb[be_i], pb[bo_i]
                    mm_e = mm_o = None
                    mm_by = {}
                    for Mt, bank, bi in [(mp, bank_e, be_i), (mp + 4, bank_o, bo_i)]:
                        slot, o_st = strips[Mt]
                        for kf in range(8):
                            dep = (tr_deps + bankg.get(bi, []) + [o_st]) if kf == 0 else []
                            o_mm = P.add("T", lambda t, kf=kf, bank=bank, slot=slot:
                                         t.matmul(bank[:, :],
                                                  dslice(wqk_s, slot * 1024 + kf * 128,
                                                         slot * 1024 + kf * 128 + 128),
                                                  dslice(ln0f, kf * 512, (kf + 1) * 512),
                                                  start=(kf == 0), stop=(kf == 7)),
                                         deps=dep, tag=f"qk{qk}mm{Mt}k{kf}m{m}")
                        mm_by[Mt] = o_mm
                        strip_guard[slot] = o_mm
                    mm_e, mm_o = mm_by[mp], mm_by[mp + 4]
                    wr0 = [rtmp_guard[0]] if rtmp_guard[0] else []
                    wr1 = [rtmp_guard[1]] if rtmp_guard[1] else []
                    o_p1 = P.add("V", lambda v, bank_e=bank_e, csb=csb: v.tensor_tensor(
                        out=dslice(rtmp, 0, 512), in0=bank_e[:, :],
                        in1=dslice(cs_s, csb, csb + 512), op=ALU.mult),
                        deps=[mm_e] + wr0 + o_cs, tag="rp1")
                    o_p2 = P.add("V", lambda v, bank_o=bank_o, csb=csb: v.tensor_tensor(
                        out=dslice(rtmp, 512, 1024), in0=bank_o[:, :],
                        in1=dslice(cs_s, csb + 512, csb + 1024), op=ALU.mult),
                        deps=[mm_o] + wr0 + o_cs, tag="rp2")
                    o_p3 = P.add("V", lambda v, bank_e=bank_e, csb=csb: v.tensor_tensor(
                        out=dslice(rtmp, 1024, 1536), in0=bank_e[:, :],
                        in1=dslice(cs_s, csb + 512, csb + 1024), op=ALU.mult),
                        deps=[mm_e] + wr1 + o_cs, tag="rp3")
                    o_p4 = P.add("V", lambda v, bank_o=bank_o, csb=csb: v.tensor_tensor(
                        out=dslice(rtmp, 1536, 2048), in0=bank_o[:, :],
                        in1=dslice(cs_s, csb, csb + 512), op=ALU.mult),
                        deps=[mm_o] + wr1 + o_cs, tag="rp4")
                    o_c1 = P.add("G", lambda g, rot=rot, mp=mp: g.tensor_tensor(
                        out=dslice(rot, mp * 512, (mp + 1) * 512), in0=dslice(rtmp, 0, 512),
                        in1=dslice(rtmp, 512, 1024), op=ALU.subtract),
                        deps=[o_p1, o_p2] + prev_qr_done, tag="rc1")
                    o_c2 = P.add("G", lambda g, rot=rot, mp=mp: g.tensor_tensor(
                        out=dslice(rot, (mp + 4) * 512, (mp + 5) * 512), in0=dslice(rtmp, 1024, 1536),
                        in1=dslice(rtmp, 1536, 2048), op=ALU.add),
                        deps=[o_p3, o_p4] + prev_qr_done, tag="rc2")
                    rtmp_guard[0] = o_c1
                    rtmp_guard[1] = o_c2
                    bankg[be_i] = [o_p1, o_p3]
                    bankg[bo_i] = [o_p2, o_p4]
            rope_all = [rtmp_guard[0], rtmp_guard[1]]

            o_vev = {}
            vparity = m % 2
            for ii in range(4):
                i = 4 * m + ii
                for nt in range(2):
                    bi = 4 + nt
                    bank = pb[bi]
                    for kf in range(8):
                        dep = (wload + tr_deps + bankg.get(bi, []) +
                               (vck_guard[ii] or [])) if kf == 0 else []
                        o_mm = P.add("T", lambda t, kf=kf, ii=ii, nt=nt, bank=bank:
                                     t.matmul(bank[:, :],
                                              dslice(ln0f, kf * 512 + ii * 128, kf * 512 + ii * 128 + 128),
                                              dslice(w_vT, kf * F + nt * 512, kf * F + nt * 512 + 512),
                                              start=(kf == 0), stop=(kf == 7)),
                                     deps=dep, tag=f"vmm{i}_{nt}")
                    o_ev = P.add("S", lambda sc, ii=ii, nt=nt, bank=bank: sc.copy(
                        out=dslice(vck, ii * 1024 + nt * 512, ii * 1024 + nt * 512 + 512),
                        in_=bank[:, :]),
                        deps=[o_mm], tag=f"vev{i}_{nt}")
                    o_vev[(ii, nt)] = o_ev
                    bankg[bi] = [o_ev]
                o_vev[(ii, "sp")] = P.add("Q", lambda s, i=i, ii=ii: s.dma_start(
                    out=v_d[128 * i:128 * (i + 1), :],
                    in_=dslice(vck, ii * 1024, ii * 1024 + 1024)),
                    deps=[o_vev[(ii, 0)], o_vev[(ii, 1)]], dma=True, tag=f"vsp{i}")
                all_spills.append(o_vev[(ii, "sp")])

            o_gev = []
            for G in range(NG):
                bank = pb[6]
                for kf in range(8):
                    dep = (wload + tr_deps + bankg.get(6, []) + prev_macro_tail) if kf == 0 else []
                    o_mm = P.add("T", lambda t, kf=kf, G=G, bank=bank:
                                 t.matmul(bank[:, :],
                                          dslice(w_gT, kf * 256 + G * 128, kf * 256 + G * 128 + 128),
                                          dslice(ln0f, kf * 512, (kf + 1) * 512),
                                          start=(kf == 0), stop=False),
                                 deps=dep, tag=f"gmm{G}k{kf}")
                o_mmb = P.add("T", lambda t, G=G, bank=bank: t.matmul(
                    bank[:, :], rawap(rows_b, 0, G * 128, [[4096, 1], [1, 128]]),
                    ones_b[:, :], start=False, stop=True), deps=[], tag=f"gbias{G}")
                o_ev = P.add("S", lambda sc, G=G, bank=bank: sc.activation(
                    out=dslice(g_fm, G * 512, (G + 1) * 512), in_=bank[:, :], func=AF.Sigmoid),
                    deps=[o_mmb], tag=f"gsig{G}")
                o_gev.append(o_ev)
                bankg[6] = [o_ev]

            o_qr_ev = []
            qrp = m % 2
            for qk, wbd, rot in [(0, w_qbd, qrot), (1, w_kbd, krot)]:
                for G in range(NG):
                    bank = pb[7]
                    for kf in range(8):
                        dep = (wload + rope_all + bankg.get(7, []) + prev_macro_tail) if kf == 0 else []
                        o_mm = P.add("T", lambda t, kf=kf, G=G, bank=bank, wbd=wbd, rot=rot:
                                     t.matmul(bank[:, :],
                                              dslice(wbd, kf * 256 + G * 128, kf * 256 + G * 128 + 128),
                                              dslice(rot, kf * 512, (kf + 1) * 512),
                                              start=(kf == 0), stop=(kf == 7)),
                                     deps=dep, tag=f"qr{qk}g{G}k{kf}")
                    if qk == 0:
                        o_ev = P.add("S", lambda sc, G=G, bank=bank, qrp=qrp: sc.copy(
                            out=dslice(qr_mb, qrp * 1024 + G * 512, qrp * 1024 + G * 512 + 512),
                            in_=bank[:, :]),
                            deps=[o_mm] + qr_sp_guard.get(qrp, []), tag=f"qrev{G}")
                    else:
                        o_ev = P.add("S", lambda sc, G=G, bank=bank: sc.copy(
                            out=dslice(kr_fm, G * 512, (G + 1) * 512), in_=bank[:, :]),
                            deps=[o_mm], tag=f"krev{G}")
                    o_qr_ev.append(o_ev)
                    bankg[7] = [o_ev]
            prev_qr_done = [o_qr_ev[-1]]
            ln0f_last_reader[m] = o_qr_ev[-1]
            qr_sp_guard[qrp] = []
            for G in range(NG):
                qr_sp_guard[qrp].append(P.add("Q", lambda s, G=G, m=m, qrp=qrp: s.dma_start(
                    out=bass.AP(qr_d.tensor, qr_d.offset + G * T + m * 512, [[NG * T, 128], [1, 512]]),
                    in_=dslice(qr_mb, qrp * 1024 + G * 512, qrp * 1024 + G * 512 + 512)),
                    deps=[o_qr_ev[G]], dma=True, tag=f"qrsp{G}m{m}"))
                all_spills.append(qr_sp_guard[qrp][-1])

            atp = m % 2
            o_at_sp = []
            for G in range(NG):
                dep = [o_gev[G], o_qr_ev[2 + G]]
                if m >= 1:
                    dep += [atq_guard.get(("afm", m - 1))]
                o_a1 = P.add("V", lambda v, G=G: v.tensor_tensor(
                    out=dslice(a_fm, G * 512, (G + 1) * 512), in0=dslice(g_fm, G * 512, (G + 1) * 512),
                    in1=dslice(kr_fm, G * 512, (G + 1) * 512), op=ALU.mult), deps=dep, tag=f"a{G}")
                o_a2 = P.add("V", lambda v, G=G, atp=atp: v.tensor_tensor(
                    out=rawap(at_mb, 0, atp * 1024 + G * 512, [["P", 128], [B, 32], [1, B]]),
                    in0=rawap(a_fm, 0, G * 512, [["P", 128], [B, 32], [1, B]]),
                    in1=rawap(tp_t, 0, G * B, [["P", 128], [0, 32], [1, B]]),
                    op=ALU.mult), deps=[o_a1] + ([atq_guard.get(("at", m - 2))] if m >= 2 else []),
                    tag=f"at{G}")
                o_a3 = P.add("G", lambda g, G=G: g.tensor_tensor(
                    out=rawap(amf, 0, G * 512, [["P", 128], [C, 4], [1, C]]),
                    in0=rawap(a_fm, 0, G * 512, [["P", 128], [C, 4], [1, C]]),
                    in1=rawap(tminfm, 0, G * C, [["P", 128], [0, 4], [1, C]]),
                    op=ALU.mult), deps=[o_a1] + ([atq_guard.get(("amf", m - 1))] if m >= 1 else []),
                    tag=f"amf{G}")
                o_sp = P.add("Q", lambda s, G=G, m=m, atp=atp: s.dma_start(
                    out=bass.AP(at_d.tensor, at_d.offset + G * T + m * 512, [[NG * T, 128], [1, 512]]),
                    in_=dslice(at_mb, atp * 1024 + G * 512, atp * 1024 + G * 512 + 512)),
                    deps=[o_a2], dma=True, tag=f"atsp{G}m{m}")
                o_at_sp.append(o_sp)
                all_spills.append(o_sp)
                atq_guard[("afm_g", m, G)] = (o_a2, o_a3)
            atq_guard[("afm", m)] = atq_guard[("afm_g", m, NG - 1)][0]
            atq_guard[("at", m)] = o_at_sp[-1]

            amp = m % 2
            o_amt = []
            for G in range(NG):
                for tb in range(4):
                    o_t = add_tr(lambda s, G=G, tb=tb, amp=amp: s.dma_start(
                        out=dslice(am_mb, amp * 1024 + tb * 256 + G * 128,
                                   amp * 1024 + tb * 256 + G * 128 + 128),
                        in_=dslice(amf, G * 512 + tb * 128, G * 512 + tb * 128 + 128),
                        transpose=True),
                        deps=[atq_guard[("afm_g", m, G)][1]] +
                             ([amb_guard[amp]] if amb_guard[amp] else []),
                        tag=f"amt{G}_{tb}")
                    o_amt.append(o_t)
            atq_guard[("amf", m)] = o_amt[-1]

            for ii in range(4):
                k = 4 * m + ii
                for G in range(NG):
                    bi = 6 + G
                    bank = pb[bi]
                    mm_list = []
                    for p4 in range(4):
                        dep = (o_amt + [o_vev[(ii, 0)], o_vev[(ii, 1)]] +
                           bankg.get(bi, [])) if p4 == 0 else []
                        mm = P.add("T", lambda t, ii=ii, G=G, p4=p4, bank=bank, amp=amp: t.matmul(
                            bank[32 * p4:32 * p4 + 32, 0:128],
                            dslice(am_mb, amp * 1024 + ii * 256 + G * 128 + 32 * p4,
                                   amp * 1024 + ii * 256 + G * 128 + 32 * p4 + 32),
                            dslice(vck, ii * 1024 + G * 512 + 128 * p4,
                                   ii * 1024 + G * 512 + 128 * p4 + 128),
                            start=True, stop=True, tile_position=(0, 32 * p4)),
                            deps=dep, tag=f"U{k}g{G}p{p4}")
                        mm_list.append(mm)
                    o_rec[G * NC + k] = P.add("V", lambda v, k=k, G=G, bank=bank: v.scalar_tensor_tensor(
                        out=rawap(Pst, 0, (G * (NC + 1) + k + 1) * 128, [["P", 128], [1, 128]]),
                        in0=rawap(Pst, 0, (G * (NC + 1) + k) * 128, [["P", 128], [1, 128]]),
                        scalar=lamC_t[:, G:G + 1],
                        in1=bank[:, 0:128], op0=ALU.mult, op1=ALU.add),
                        deps=mm_list + ([o_rec[G * NC + k - 1]] if k >= 1 else []),
                        tag=f"rec{k}g{G}")
                    bankg[bi] = [o_rec[G * NC + k]]
                vck_guard[ii] = [o_rec[1 * NC + k], o_vev[(ii, "sp")]]
            amb_guard[amp] = o_rec[1 * NC + 4 * m + 3]
            atq_guard[("cs", m)] = [rtmp_guard[0], rtmp_guard[1]]
            prev_macro_tail = [o_rec[0 * NC + 4 * m + 3], o_rec[1 * NC + 4 * m + 3]]

        P.emit_block()
        if max_phase < 2:
            return nc

        o_pay = []
        for G in range(NG):
            o_pay.append(P.add("Q", lambda s, G=G: s.dma_start(
                out=cc_in.ap()[G * 128:(G + 1) * 128, :],
                in_=rawap(Pst, 0, (G * (NC + 1) + NC) * 128, [["P", 128], [1, 128]])),
                dma=True, tag=f"pay{G}"))
        o_cc = P.add("G", lambda g: g.collective_compute(
            "AllGather", ALU.bypass, replica_groups=[list(range(n_ranks))],
            ins=[cc_in.ap().opt()], outs=[cc_out.ap().opt()]),
            deps=o_pay, tag="allgather", coll=True)
        o_ccl = P.add("Q", lambda s: s.dma_start(
            out=ccg.sl(0, n_ranks * NG * 128),
            in_=bass.AP(cc_out.ap().tensor, cc_out.ap().offset,
                        [[128, 128], [128 * 128, n_ranks * NG], [1, 128]])),
            deps=[o_cc], dma=True, tag="ccload")
        o_fin = [None] * (NG * NC)
        for G in range(NG):
            o_z = P.add("V", lambda v, G=G: v.memset(
                rawap(Msta, 0, G * 256, [["P", 128], [1, 128]]), 0.0), tag=f"msz{G}")
            prev = o_z
            cur = 0
            for j in range(n_ranks):
                prev = P.add("V", lambda v, G=G, j=j, cur=cur: v.scalar_tensor_tensor(
                    out=rawap(Msta, 0, G * 256 + (1 - cur) * 128, [["P", 128], [1, 128]]),
                    in0=ccg.ap(0, (j * NG + G) * 128, [["P", 128], [1, 128]]),
                    scalar=wj_t[:, G * 8 + j:G * 8 + j + 1],
                    in1=rawap(Msta, 0, G * 256 + cur * 128, [["P", 128], [1, 128]]),
                    op0=ALU.mult, op1=ALU.add),
                    deps=[prev, o_ccl], tag=f"ms{G}_{j}")
                cur = 1 - cur
            for k in range(NC):
                o_fin[G * NC + k] = P.add("V", lambda v, G=G, k=k, cur=cur: v.scalar_tensor_tensor(
                    out=rawap(Pst, 0, (G * (NC + 1) + k) * 128, [["P", 128], [1, 128]]),
                    in0=rawap(Msta, 0, G * 256 + cur * 128, [["P", 128], [1, 128]]),
                    scalar=tk_t[:, G * NC + k:G * NC + k + 1],
                    in1=rawap(Pst, 0, (G * (NC + 1) + k) * 128, [["P", 128], [1, 128]]),
                    op0=ALU.mult, op1=ALU.add),
                    deps=[prev] + ([o_pay[G]] if k == 0 else []), tag=f"fin{G}_{k}")
        P.emit_block()
        if max_phase < 3:
            return nc

        o_zbd = [
            P.add("V", lambda v: v.memset(abd.sl(0, 2 * NG * 1024), 0.0), tag="zabd"),
            P.add("V", lambda v: v.memset(vbd.sl(0, 2 * NG * 4096), 0.0), tag="zvbd"),
            P.add("V", lambda v: v.memset(mkbd.sl(0, 8192), 0.0), tag="zmkbd"),
        ]
        o_ysp = [None] * NC
        pc = {}
        for k in range(NC):
            kb = k % 2
            o_qrl = P.add("Q", lambda s, k=k, kb=kb: s.dma_start(
                out=dslice(qrc, kb * 256, kb * 256 + 256),
                in_=bass.AP(qr_d.tensor, qr_d.offset + k * C,
                            [[NG * T, 128], [T, NG], [1, C]])),
                deps=pc.get(("qtqp", k - 2), []) + (all_spills if k == 0 else []),
                dma=True, tag=f"qrl{k}")
            if k % 4 == 0:
                bnum = k // 4
                bpar = bnum % 2
                nk4 = min(4, NC - k)
                mkb_build = []
                war_mk = ([pc.get(("ymm", 4 * (bnum - 2) + 3))] if bnum >= 2
                          else list(o_zbd))
                for G in range(NG):
                    findeps = [o_fin[G * NC + k + kk4] for kk4 in range(nk4)]
                    for p4 in range(4):
                        eng = "V"
                        def fmk(e, G=G, p4=p4, k=k, bpar=bpar, nk4=nk4):
                            s_ap = rawap(Pst, 32 * p4, (G * (NC + 1) + k) * 128,
                                         [["P", 32], [128, nk4], [1, 128]])
                            t_ap = mkbd.ap(32 * p4, bpar * 4096 + G * 512 + 128 * p4,
                                           [["P", 32], [1024, nk4], [1, 128]])
                            m_ap = rawap(blk_m, 32 * p4, 0,
                                         [["P", 32], [0, nk4], [1, 128]])
                            return e.tensor_tensor(out=t_ap, in0=s_ap, in1=m_ap,
                                                   op=ALU.mult)
                        mkb_build.append(P.add(eng, fmk,
                            deps=findeps + war_mk, tag=f"mkbd{k}g{G}p{p4}"))
                pc[("mkbd", bnum)] = mkb_build
            o_mkb = pc[("mkbd", k // 4)]
            bpar_k = (k // 4) % 2
            o_qt = []
            o_qp = []
            for G in range(NG):
                dep = [o_qrl] + ([pc[("smm", k - 1, G)]] if k >= 1 else [])
                o_qt.append(P.add("V", lambda v, G=G, kb=kb: v.tensor_tensor(
                    out=rawap(qt_b, 0, G * 8 * C, [["P", 128], [C, 8], [1, C]]),
                    in0=rawap(qrc, 0, kb * 256 + G * C, [["P", 128], [0, 8], [1, C]]),
                    in1=rawap(tlam, 0, G * 8 * C, [["P", 128], [C, 8], [1, C]]),
                    op=ALU.mult), deps=dep, tag=f"qt{k}g{G}"))
                o_qp.append(P.add("G", lambda g, G=G, kb=kb: g.tensor_tensor(
                    out=dslice(qp_b, G * C, (G + 1) * C),
                    in0=dslice(qrc, kb * 256 + G * C, kb * 256 + (G + 1) * C),
                    in1=tplus[:, G * C:(G + 1) * C], op=ALU.mult),
                    deps=[o_qrl] + ([pc[("ymm", k - 1)]] if k >= 1 else []), tag=f"qp{k}g{G}"))
            pc[("qtqp", k)] = [o_qt[-1], o_qp[-1]]
            o_bd = []
            war_bd = ([pc[("ymm", k - 2)]] if k >= 2 else []) + \
                (all_spills + o_zbd if k == 0 else [])
            for G in range(NG):
                for hl in range(8):
                    def fa(s, G=G, hl=hl, k=k, kb=kb):
                        t_ap = rawap(abd, 16 * hl, kb * 2048 + G * 1024 + 16 * hl,
                                     [["P", 16], [128, 8], [1, 16]])
                        s_ap = bass.AP(at_d.tensor, at_d.offset + (16 * hl) * NG * T + G * T + k * C,
                                       [[NG * T, 16], [16, 8], [1, 16]])
                        return s.dma_start(out=t_ap, in_=s_ap)
                    if "bd" in STUB:
                        def fa(s, G=G, hl=hl, k=k, kb=kb):
                            t_ap = rawap(abd, 16 * hl, kb * 2048 + G * 1024,
                                         [["P", 16], [1, 128]])
                            s_ap = bass.AP(at_d.tensor, at_d.offset + (16 * hl) * NG * T,
                                           [[NG * T, 16], [1, 128]])
                            return s.dma_start(out=t_ap, in_=s_ap)
                    o_bd.append(P.add("Q", fa, deps=war_bd, dma=True, tag=f"abd{k}g{G}h{hl}"))
                    def fv(s, G=G, hl=hl, k=k, kb=kb):
                        t_ap = rawap(vbd, 16 * hl, kb * 8192 + G * 4096 + 64 * hl,
                                     [["P", 16], [512, 8], [1, 64]])
                        s_ap = bass.AP(v_d.tensor, v_d.offset + (k * C) * F + G * 512 + 64 * hl,
                                       [[F, 16], [16 * F, 8], [1, 64]])
                        return s.dma_start(out=t_ap, in_=s_ap)
                    if "bd" in STUB:
                        def fv(s, G=G, hl=hl, k=k, kb=kb):
                            t_ap = rawap(vbd, 16 * hl, kb * 8192 + G * 4096,
                                         [["P", 16], [1, 512]])
                            s_ap = bass.AP(v_d.tensor, v_d.offset + (k * C) * F,
                                           [[F, 16], [1, 512]])
                            return s.dma_start(out=t_ap, in_=s_ap)
                    o_bd.append(P.add("Q", fv, deps=war_bd, dma=True, tag=f"vbd{k}g{G}h{hl}"))
            o_smm = {}
            for G in range(NG):
                last = None
                for j in range(8):
                    bank = pb[G * 2 + j // 4]
                    dep = (o_bd + o_qt +
                           ([pc[("sev", k - 1, G)]] if k >= 1 else [])) if j == 0 else o_bd
                    if "smm" in STUB:
                        last = P.add("T", lambda t, G=G, j=j, kb=kb, bank=bank:
                                     t.matmul(bank[:, (j % 4) * 128:(j % 4) * 128 + C],
                                              dslice(abd, kb * 2048 + G * 1024 + j * 128,
                                                     kb * 2048 + G * 1024 + j * 128 + 128),
                                              dslice(qt_b, G * 8 * C + j * C,
                                                     G * 8 * C + (j + 1) * C),
                                              start=True, stop=True),
                                     deps=dep, tag=f"smm{k}g{G}j{j}")
                    else:
                        last = P.add("T", lambda t, G=G, j=j, kb=kb, bank=bank:
                                     t.matmul(bank[:, (j % 4) * 128 + B * j:(j % 4) * 128 + C],
                                              dslice(abd, kb * 2048 + G * 1024 + j * 128,
                                                     kb * 2048 + G * 1024 + j * 128 + 128),
                                              dslice(qt_b, G * 8 * C + j * C + B * j,
                                                     G * 8 * C + (j + 1) * C),
                                              start=True, stop=True),
                                     deps=dep, tag=f"smm{k}g{G}j{j}")
                o_smm[G] = last
                pc[("smm", k, G)] = last
            o_sev = {}
            for G in range(NG):
                evs = []
                for half in range(2):
                    bank = pb[G * 2 + half]
                    evs.append(P.add("V", lambda v, G=G, half=half, bank=bank: v.tensor_tensor(
                        out=dslice(ssb, G * 8 * C + half * 512, G * 8 * C + half * 512 + 512),
                        in0=bank[:, :], in1=tri_m[:, half * 512:half * 512 + 512],
                        op=ALU.mult),
                        deps=[o_smm[G]] + ([pc[("ymm", k - 1)]] if k >= 1 else []),
                        tag=f"sev{k}g{G}h{half}"))
                o_sev[G] = evs
                pc[("sev", k, G)] = evs[-1]
            o_ymm = []
            for G in range(NG):
                bank = pb[6 + G]
                last = None
                for j in range(8):
                    dep = (o_sev[G] + o_bd +
                           ([pc[("yev", k - 1, G)]] if k >= 1 else [])) if j == 0 else []
                    last = P.add("T", lambda t, G=G, j=j, kb=kb, bank=bank:
                                 t.matmul(bank[:, :],
                                          dslice(ssb, G * 8 * C + j * 128, G * 8 * C + j * 128 + 128),
                                          dslice(vbd, kb * 8192 + G * 4096 + j * 512,
                                                 kb * 8192 + G * 4096 + (j + 1) * 512),
                                          start=(j == 0), stop=False),
                                 deps=dep, tag=f"yin{k}g{G}j{j}")
                last = P.add("T", lambda t, G=G, bank=bank, k=k, bpar_k=bpar_k:
                             t.matmul(bank[:, :],
                                      qp_b.sl(G * C, (G + 1) * C),
                                      mkbd.sl(bpar_k * 4096 + (k % 4) * 1024 + G * 512,
                                              bpar_k * 4096 + (k % 4) * 1024 + G * 512 + 512),
                                      start=False, stop=True),
                             deps=[o_qp[G]] + o_mkb, tag=f"yie{k}g{G}")
                o_ymm.append(last)
            pc[("ymm", k)] = o_ymm[-1]
            o_yev = []
            for G in range(NG):
                o_yev.append(P.add("S", lambda sc, G=G: sc.copy(
                    out=lntm[:, G * 512:(G + 1) * 512], in_=pb[6 + G][:, :]),
                    deps=[o_ymm[G]] + (pc[("ytr", k - 1)] if k >= 1 else []),
                    tag=f"yev{k}g{G}"))
                pc[("yev", k, G)] = o_yev[-1]
            o_ytr = []
            for fb in range(8):
                o_ytr.append(add_tr(lambda e, fb=fb: e.dma_start(
                    out=yfmc[:, fb * 128:fb * 128 + 128],
                    in_=lntm[:, fb * 128:fb * 128 + 128], transpose=True),
                    deps=o_yev + ([o_ysp[k - 1]] if k >= 1 else []),
                    tag=f"ytr{k}_{fb}"))
            pc[("ytr", k)] = o_ytr
            o_ysp[k] = P.add("Q", lambda s, k=k: s.dma_start(
                out=bass.AP(yfm_d.tensor, yfm_d.offset + k * 1024, [[T * 8, 128], [1, 1024]]),
                in_=yfmc[:, 0:1024]), deps=o_ytr, dma=True, tag=f"ysp{k}")
        P.emit_block()
        if max_phase < 4:
            return nc

        o_wol = []
        for kf in range(8):
            o_wol.append(P.add("G", lambda g, kf=kf: g.dma_start(
                out=dslice(w_oT, F * kf, F * (kf + 1)),
                in_=owT_d[128 * kf:128 * (kf + 1), :]), dma=True, tag=f"wol{kf}"))
        o_lng = bcast_load(lng_r, lng_d, F)
        o_lnb = bcast_load(lnb_r, lnb_d, F)
        o_scale = {}
        o_wq_last = {}
        fqs = tv(28672, 4096)

        def fq_gen():
            prev_tail = None
            for wname, src_d, ntile, ncol, rng in [("fc", wfcT_d, 8, FF, "G"),
                                                   ("pr", wprT_d, 32, F, "Q")]:
                prev_ld = prev_tail
                o_acc_last = None
                for kt in range(ntile):
                    o_ld = P.add(rng, lambda s, src_d=src_d, kt=kt, ncol=ncol: s.dma_start(
                        out=rawap(Pst, 0, 4096, [["P", 128], [1, ncol]]),
                        in_=src_d[128 * kt:128 * (kt + 1), :]),
                        deps=([prev_ld] if prev_ld else []),
                        dma=True, tag=f"wld{wname}{kt}")
                    yield
                    o_m = P.add("V", lambda v, ncol=ncol: v.tensor_reduce(
                        out=rawap(st4, 0, 14, [["P", 128], [1, 1]]),
                        in_=rawap(Pst, 0, 4096, [["P", 128], [1, ncol]]),
                        axis=mybir.AxisListType.X, op=ALU.max, apply_absolute_value=True),
                        deps=[o_ld], tag=f"wmx{wname}{kt}")
                    yield
                    o_acc = P.add("V", lambda v, kt=kt: v.tensor_tensor(
                        out=rawap(st4, 0, 15, [["P", 128], [1, 1]]),
                        in0=(rawap(st4, 0, 15, [["P", 128], [1, 1]]) if kt > 0
                             else rawap(st4, 0, 14, [["P", 128], [1, 1]])),
                        in1=rawap(st4, 0, 14, [["P", 128], [1, 1]]), op=ALU.max),
                        deps=[o_m], tag=f"wacc{wname}{kt}")
                    yield
                    prev_ld = o_acc
                    o_acc_last = o_acc
                o_tp1 = P.add("Q", lambda s: s.dma_start(
                    out=rawap(scr, 0, 64, [["P", 1], [1, 128]]),
                    in_=rawap(st4, 0, 15, [["P", 128], [1, 1]])),
                    deps=[o_acc_last], dma=True, tag=f"wtr{wname}")
                yield
                o_gm = P.add("V", lambda v: v.tensor_reduce(
                    out=rawap(scr, 0, 192, [["P", 1], [1, 1]]),
                    in_=rawap(scr, 0, 64, [["P", 1], [1, 128]]),
                    axis=mybir.AxisListType.X, op=ALU.max), deps=[o_tp1], tag=f"wgm{wname}")
                o_sc = P.add("V", lambda v: v.tensor_scalar(
                    out=rawap(scr, 0, 193, [["P", 1], [1, 1]]), in0=rawap(scr, 0, 192, [["P", 1], [1, 1]]),
                    scalar1=1.0 / 127.0, scalar2=0.0, op0=ALU.mult, op1=ALU.add), deps=[o_gm], tag=f"wsc{wname}")
                o_rc = P.add("V", lambda v: v.reciprocal(
                    out=rawap(scr, 0, 194, [["P", 1], [1, 1]]), in_=rawap(scr, 0, 193, [["P", 1], [1, 1]])),
                    deps=[o_sc], tag=f"wrc{wname}")
                yield
                o_b1 = P.add("Q", lambda s: s.dma_start(
                    out=bass.AP(srow_d.ap().tensor, srow_d.ap().offset, [[1, 1], [1, 2]]),
                    in_=rawap(scr, 0, 193, [["P", 1], [1, 2]])), deps=[o_rc], dma=True, tag=f"wb1{wname}")
                yield
                col = 200 if wname == "fc" else 204
                o_b2 = P.add("Q", lambda s, col=col: s.dma_start(
                    out=rawap(scr, 0, col, [["P", 128], [1, 2]]),
                    in_=bass.AP(srow_d.ap().tensor, srow_d.ap().offset, [[0, 128], [1, 2]])),
                    deps=[o_b1], dma=True, tag=f"wb2{wname}")
                yield
                o_scale[wname] = o_b2
                prev_tail = o_b2

            prev = o_scale["pr"]
            for wname, src_d, dst_d, ntile, ncol, rng in [("fc", wfcT_d, wqf_d, 8, FF, "G"),
                                                          ("pr", wprT_d, wqp_d, 32, F, "Q")]:
                col = 200 if wname == "fc" else 204
                nh = ncol // 1024 if ncol > 1024 else 1
                hw_ = ncol // nh
                dpitch = ntile * ncol
                for kt in range(ntile):
                    for hh in range(nh):
                        o_ld = P.add(rng, lambda s, src_d=src_d, kt=kt, hh=hh, hw_=hw_: s.dma_start(
                            out=rawap(Pst, 0, 4096, [["P", 128], [1, hw_]]),
                            in_=src_d[128 * kt:128 * (kt + 1), hh * hw_:(hh + 1) * hw_]),
                            deps=([prev] if prev else []),
                            dma=True, tag=f"wl2{wname}{kt}_{hh}")
                        yield
                        o_q1 = P.add("V", lambda v, hw_=hw_, col=col: v.tensor_scalar(
                            out=ki32[:, 0:hw_],
                            in0=rawap(Pst, 0, 4096, [["P", 128], [1, hw_]]),
                            scalar1=rawap(scr, 0, col + 1, [["P", 128], [1, 1]]), scalar2=0.0,
                            op0=ALU.mult, op1=ALU.add), deps=[o_ld, o_scale[wname]],
                            tag=f"q1{wname}{kt}_{hh}")
                        yield
                        o_q4 = P.add("V", lambda v, hw_=hw_, hh=hh, col=col: v.tensor_scalar(
                            out=rawap(fqs, 0, hh * hw_, [["P", 128], [1, hw_]]),
                            in0=ki32[:, 0:hw_],
                            scalar1=-128.0, scalar2=rawap(scr, 0, col, [["P", 128], [1, 1]]),
                            op0=ALU.max, op1=ALU.mult), deps=[o_q1], tag=f"q4{wname}{kt}_{hh}")
                        yield
                        prev = o_q4
                    o_sp = P.add(rng, lambda s, dst_d=dst_d, kt=kt, ncol=ncol, dpitch=dpitch:
                                 s.dma_start(
                                     out=bass.AP(dst_d.tensor, dst_d.offset + kt * ncol,
                                                 [[dpitch, 128], [1, ncol]]),
                                     in_=rawap(fqs, 0, 0, [["P", 128], [1, ncol]])),
                                 deps=[prev], dma=True, tag=f"wsp{wname}{kt}")
                    yield
                    prev = o_sp
                o_wq_last[wname] = prev

        _fq = fq_gen()

        def fq_drip(n):
            for _ in range(n):
                try:
                    next(_fq)
                except StopIteration:
                    return

        o_xn = [None] * NTT
        prev_b4 = {}
        for i in range(NTT):
            par = i % 2
            sb = 16 * par
            s_ln0 = 2048 * par
            s_hr = 2048 * par + 1024
            o_x2 = P.add("Q", lambda s, i=i, par=par: s.dma_start(
                out=x_t[:, par * F:(par + 1) * F],
                in_=x_in[128 * i:128 * (i + 1), :]),
                deps=[o_xn[i - 2]] if i >= 2 else [], dma=True, tag=f"x2_{i}")
            o_yl = P.add("Q", lambda s, i=i, par=par: s.dma_start(
                out=yfmc[:, par * 1024:(par + 1) * 1024],
                in_=bass.AP(yfm_d.tensor, yfm_d.offset + i * 1024, [[T * 8, 128], [1, 1024]])),
                deps=[o_ysp[i]] + ([o_ysp[NC - 1]] if i == 0 else []) +
                     (prev_b4.get(("omm", i - 2), [])),
                dma=True, tag=f"yl{i}")
            o_omm = []
            for nt in range(2):
                bank = pb[2 * par + nt]
                for kf in range(8):
                    dep = ([o_yl] + o_wol +
                           ([prev_b4.get(("hres", i - 2, nt))] if i >= 2 else [])) if kf == 0 else []
                    P.add("T", lambda t, kf=kf, nt=nt, bank=bank, par=par: t.matmul(
                        bank[:, :], yfmc[:, par * 1024 + kf * 128:par * 1024 + kf * 128 + 128],
                        dslice(w_oT, kf * F + nt * 512, kf * F + nt * 512 + 512),
                        start=(kf == 0), stop=False), deps=dep, tag=f"omm{i}_{nt}k{kf}")
                last = P.add("T", lambda t, nt=nt, bank=bank: t.matmul(
                    bank[:, :], ones_b[:, 0:128],
                    rawap(rows_b, 0, 256 + nt * 512, [[4096, 1], [1, 512]]),
                    start=False, stop=True), deps=[], tag=f"obias{i}_{nt}")
                o_omm.append(last)
            prev_b4[("omm", i)] = o_omm
            o_l0 = P.add("V", lambda v, i=i, par=par, s_ln0=s_ln0: v.scalar_tensor_tensor(
                out=rawap(stg, 0, s_ln0, [["P", 128], [1, F]]),
                in0=x_t[:, par * F:(par + 1) * F], scalar=mu_s[:, i:i + 1],
                in1=bass.AP(rs_s[:].tensor, rs_s[:].offset + i, [[NTT, 128], [0, F]]),
                op0=ALU.subtract, op1=ALU.mult),
                deps=[o_x2] + ([prev_b4.get(("g1", i - 2))] if i >= 2 else []), tag=f"l0r{i}")
            o_hr = []
            for nt in range(2):
                o_hr.append(P.add("V", lambda v, nt=nt, par=par, s_ln0=s_ln0, s_hr=s_hr: v.tensor_tensor(
                    out=rawap(stg, 0, s_hr + nt * 512, [["P", 128], [1, 512]]),
                    in0=rawap(stg, 0, s_ln0 + nt * 512, [["P", 128], [1, 512]]),
                    in1=pb[2 * par + nt][:, :], op=ALU.add),
                    deps=[o_l0, o_omm[nt]] + ([prev_b4.get(("xn", i - 2))] if i >= 2 else []),
                    tag=f"hres{i}_{nt}"))
                prev_b4[("hres", i, nt)] = o_hr[-1]
            hres = rawap(stg, 0, s_hr, [["P", 128], [1, F]])
            stat_war = ([prev_b4.get(("nrm", i - 2)), prev_b4.get(("rs2", i - 2))]
                        if i >= 2 else [])
            o_s1 = P.add("S", lambda sc, hres=hres, sb=sb: sc.activation(
                out=lntm[:, 0:F], in_=hres, func=AF.Copy,
                accum_out=rawap(st4, 0, sb + 0, [["P", 128], [1, 1]])),
                deps=o_hr + stat_war + [dumg[0]], tag=f"h1s{i}")
            dumg[0] = o_s1
            o_s2 = P.add("S", lambda sc, hres=hres, sb=sb: sc.activation(
                out=lntm[:, 0:F], in_=hres, func=AF.Square,
                accum_out=rawap(st4, 0, sb + 1, [["P", 128], [1, 1]])),
                deps=o_hr + stat_war + [dumg[0]], tag=f"h2s{i}")
            dumg[0] = o_s2
            o_v1 = P.add("V", lambda v, sb=sb: v.tensor_scalar(
                out=rawap(st4, 0, sb + 2, [["P", 128], [1, 1]]), in0=rawap(st4, 0, sb + 0, [["P", 128], [1, 1]]),
                scalar1=1.0 / F, scalar2=0.0, op0=ALU.mult, op1=ALU.add), deps=[o_s1], tag=f"hmu{i}")
            o_v2 = P.add("V", lambda v, sb=sb: v.tensor_scalar(
                out=rawap(st4, 0, sb + 3, [["P", 128], [1, 1]]), in0=rawap(st4, 0, sb + 1, [["P", 128], [1, 1]]),
                scalar1=1.0 / F, scalar2=EPS, op0=ALU.mult, op1=ALU.add), deps=[o_s2], tag=f"hs2{i}")
            o_v3 = P.add("V", lambda v, sb=sb: v.tensor_tensor(
                out=rawap(st4, 0, sb + 4, [["P", 128], [1, 1]]), in0=rawap(st4, 0, sb + 2, [["P", 128], [1, 1]]),
                in1=rawap(st4, 0, sb + 2, [["P", 128], [1, 1]]), op=ALU.mult), deps=[o_v1], tag=f"hm2{i}")
            o_v4 = P.add("V", lambda v, sb=sb: v.tensor_tensor(
                out=rawap(st4, 0, sb + 5, [["P", 128], [1, 1]]), in0=rawap(st4, 0, sb + 3, [["P", 128], [1, 1]]),
                in1=rawap(st4, 0, sb + 4, [["P", 128], [1, 1]]), op=ALU.subtract), deps=[o_v2, o_v3], tag=f"hv{i}")
            o_v5 = P.add("V", lambda v, sb=sb: v.reciprocal(
                out=rawap(st4, 0, sb + 6, [["P", 128], [1, 1]]), in_=rawap(st4, 0, sb + 5, [["P", 128], [1, 1]])),
                deps=[o_v4], tag=f"hr{i}")
            o_rsx = P.add("S", lambda sc, sb=sb: sc.activation(
                out=rawap(st4, 0, sb + 7, [["P", 128], [1, 1]]), in_=rawap(st4, 0, sb + 6, [["P", 128], [1, 1]]),
                func=AF.Sqrt), deps=[o_v5], tag=f"hrs{i}")
            o_nrm = P.add("V", lambda v, hres=hres, sb=sb, s_ln0=s_ln0: v.scalar_tensor_tensor(
                out=rawap(stg, 0, s_ln0, [["P", 128], [1, F]]), in0=hres,
                scalar=rawap(st4, 0, sb + 2, [["P", 128], [1, 1]]),
                in1=rawap(st4, 0, sb + 7, [["P", 128], [0, F]]),
                op0=ALU.subtract, op1=ALU.mult), deps=[o_rsx] + o_hr, tag=f"nrm{i}")
            prev_b4[("nrm", i)] = o_nrm
            o_g1 = P.add("G", lambda g, s_ln0=s_ln0, s_hr=s_hr: g.tensor_tensor(
                out=rawap(stg, 0, s_hr, [["P", 128], [1, F]]),
                in0=rawap(stg, 0, s_ln0, [["P", 128], [1, F]]), in1=lng_r.sl(0, 1024),
                op=ALU.mult), deps=[o_nrm, o_s1, o_s2, o_lng], tag=f"g1_{i}")
            prev_b4[("g1", i)] = o_g1
            o_g2 = P.add("G", lambda g, s_hr=s_hr: g.tensor_tensor(
                out=rawap(stg, 0, s_hr, [["P", 128], [1, F]]),
                in0=rawap(stg, 0, s_hr, [["P", 128], [1, F]]), in1=lnb_r.sl(0, 1024),
                op=ALU.add), deps=[o_g1, o_lnb], tag=f"g2_{i}")
            o_g3 = P.add("G", lambda g, par=par, s_hr=s_hr: g.tensor_tensor(
                out=rawap(stg, 0, s_hr, [["P", 128], [1, F]]),
                in0=rawap(stg, 0, s_hr, [["P", 128], [1, F]]),
                in1=x_t[:, par * F:(par + 1) * F], op=ALU.add), deps=[o_g2], tag=f"g3_{i}")
            xnew = rawap(stg, 0, s_hr, [["P", 128], [1, F]])
            o_t1 = P.add("S", lambda sc, xnew=xnew, sb=sb: sc.activation(
                out=lntm[:, 0:F], in_=xnew, func=AF.Copy,
                accum_out=rawap(st4, 0, sb + 8, [["P", 128], [1, 1]])), deps=[o_g3, dumg[0]], tag=f"t1_{i}")
            dumg[0] = o_t1
            o_t2 = P.add("S", lambda sc, xnew=xnew, sb=sb: sc.activation(
                out=lntm[:, 0:F], in_=xnew, func=AF.Square,
                accum_out=rawap(st4, 0, sb + 9, [["P", 128], [1, 1]])), deps=[o_g3, dumg[0]], tag=f"t2_{i}")
            dumg[0] = o_t2
            o_w1 = P.add("V", lambda v, i=i, sb=sb: v.tensor_scalar(
                out=mu2_s[:, i:i + 1], in0=rawap(st4, 0, sb + 8, [["P", 128], [1, 1]]),
                scalar1=1.0 / F, scalar2=0.0, op0=ALU.mult, op1=ALU.add), deps=[o_t1], tag=f"w1_{i}")
            o_w2 = P.add("V", lambda v, sb=sb: v.tensor_scalar(
                out=rawap(st4, 0, sb + 10, [["P", 128], [1, 1]]), in0=rawap(st4, 0, sb + 9, [["P", 128], [1, 1]]),
                scalar1=1.0 / F, scalar2=EPS, op0=ALU.mult, op1=ALU.add), deps=[o_t2], tag=f"w2_{i}")
            o_w3 = P.add("V", lambda v, i=i, sb=sb: v.tensor_tensor(
                out=rawap(st4, 0, sb + 11, [["P", 128], [1, 1]]), in0=mu2_s[:, i:i + 1],
                in1=mu2_s[:, i:i + 1], op=ALU.mult), deps=[o_w1], tag=f"w3_{i}")
            o_w4 = P.add("V", lambda v, sb=sb: v.tensor_tensor(
                out=rawap(st4, 0, sb + 12, [["P", 128], [1, 1]]), in0=rawap(st4, 0, sb + 10, [["P", 128], [1, 1]]),
                in1=rawap(st4, 0, sb + 11, [["P", 128], [1, 1]]), op=ALU.subtract), deps=[o_w2, o_w3], tag=f"w4_{i}")
            o_w5 = P.add("V", lambda v, sb=sb: v.reciprocal(
                out=rawap(st4, 0, sb + 13, [["P", 128], [1, 1]]), in_=rawap(st4, 0, sb + 12, [["P", 128], [1, 1]])),
                deps=[o_w4], tag=f"w5_{i}")
            o_rs2 = P.add("S", lambda sc, i=i, sb=sb: sc.activation(
                out=rs2_s[:, i:i + 1], in_=rawap(st4, 0, sb + 13, [["P", 128], [1, 1]]),
                func=AF.Sqrt), deps=[o_w5], tag=f"rs2_{i}")
            prev_b4[("rs2", i)] = o_rs2
            o_xn[i] = P.add("Q", lambda s, i=i, xnew=xnew: s.dma_start(
                out=xnew_d[128 * i:128 * (i + 1), :], in_=xnew),
                deps=[o_t1, o_t2, o_rs2], dma=True, tag=f"xnsp{i}")
            prev_b4[("xn", i)] = o_xn[i]
            fq_drip(10)
        fq_drip(10000)
        P.emit_block()
        if max_phase < 5:
            return nc

        o_out_all = []
        ring_guard_fc = {}
        ring_guard_pj = {}
        prev_fc_last = None
        tr_by_tile = {}
        prev_proj = {}
        prev_xuse = {}
        prev_gel = {}
        for mq in range(NM5):
            o_tr2 = []
            for ii in range(2):
                i = 2 * mq + ii
                dep = [o_wq_last["pr"], o_xn[i]]
                if (mq - 1, ii) in prev_xuse:
                    dep.append(prev_xuse[(mq - 1, ii)])
                o_xl = P.add("Q", lambda s, i=i, ii=ii: s.dma_start(
                    out=x_t[:, (ii % 2) * F:(ii % 2 + 1) * F],
                    in_=xnew_d[128 * i:128 * (i + 1), :]), deps=dep, dma=True, tag=f"xnl{i}")
                lntm_war = ([prev_fc_last] if prev_fc_last else []) + [o_wq_last["pr"]] + tr_by_tile.get((mq - 1, ii), [])
                o_l2 = P.add("V", lambda v, i=i, ii=ii: v.scalar_tensor_tensor(
                    out=lntm[:, (ii % 2) * F:(ii % 2 + 1) * F],
                    in0=x_t[:, (ii % 2) * F:(ii % 2 + 1) * F], scalar=mu2_s[:, i:i + 1],
                    in1=bass.AP(rs2_s[:].tensor, rs2_s[:].offset + i, [[NTT, 128], [0, F]]),
                    op0=ALU.subtract, op1=ALU.mult),
                    deps=[o_xl] + lntm_war, tag=f"ln2_{i}")
                tilet = []
                for fb in range(8):
                    war = [prev_fc_last] if prev_fc_last else []
                    tilet.append(add_tr(lambda e, ii=ii, fb=fb: e.dma_start(
                        out=dslice(ln2f, fb * 256 + ii * 128, fb * 256 + ii * 128 + 128),
                        in_=lntm[:, (ii % 2) * F + fb * 128:(ii % 2) * F + fb * 128 + 128],
                        transpose=True), deps=[o_l2] + war, tag=f"tr2_{i}_{fb}"))
                o_tr2 += tilet
                tr_by_tile[(mq, ii)] = tilet
            if ln2f_d is not None:
                P.add("Q", lambda s, mq=mq: s.dma_start(
                    out=bass.AP(ln2f_d.tensor, ln2f_d.offset + mq * 2048,
                                [[(T // 256) * 2048, 128], [1, 2048]]),
                    in_=ln2f.sl(0, 2048)), deps=list(o_tr2), dma=True, tag=f"dln2{mq}")
            o_lmm = None
            for kf in range(8):
                dep = (o_tr2 + wload + ([prev_gel.get(mq - 1)] if mq >= 1 else [])) if kf == 0 else []
                o_lmm = P.add("T", lambda t, kf=kf: t.matmul(
                    pb[4][0:16, 0:256], lafc[:, kf * 16:(kf + 1) * 16],
                    dslice(ln2f, kf * 256, (kf + 1) * 256),
                    start=(kf == 0), stop=(kf == 7)), deps=dep, tag=f"l1mm{mq}k{kf}")
            o_l1e = P.add("S", lambda sc: sc.mul(
                out=rawap(lora_b, 0, 0, [["P", 16], [1, 256]]), in_=pb[4][0:16, 0:256], mul=2.0),
                deps=[o_lmm], tag=f"l1ev{mq}")
            o_gel = []
            for ft in range(32):
                slot = ft % 8
                o_ws = P.add("Q", lambda s, ft=ft, slot=slot: s.dma_start(
                    out=dslice(fcW, slot * 1024, (slot + 1) * 1024),
                    in_=bass.AP(wqf_d.tensor, wqf_d.offset + ft * 128,
                                [[8 * FF, 128], [FF, 8], [1, 128]])),
                    deps=[o_wq_last["fc"], o_wq_last["pr"]] +
                         ([ring_guard_fc.get(slot)] if ring_guard_fc.get(slot) else []),
                    dma=True, tag=f"fws{mq}f{ft}")
                bank = pb[ft % 2]
                prevg = o_gel[ft - 2] if ft >= 2 else prev_proj.get(mq - 1)
                last = None
                for kf in range(8):
                    dep = ([o_l1e, o_ws] + o_tr2 + ([prevg] if prevg else [])) if kf == 0 else []
                    last = P.add("T", lambda t, kf=kf, slot=slot, bank=bank: t.matmul(
                        bank[:, 0:256], dslice(fcW, slot * 1024 + kf * 128, slot * 1024 + kf * 128 + 128),
                        dslice(ln2f, kf * 256, (kf + 1) * 256),
                        start=(kf == 0), stop=False), deps=dep, tag=f"fc{mq}f{ft}k{kf}")
                last = P.add("T", lambda t, ft=ft, bank=bank: t.matmul(
                    bank[:, 0:256], bass.AP(lbfcr[:].tensor, lbfcr[:].offset + ft * 128,
                                            [[FF, 1], [1, 128]]),
                    ones_b[:, 0:256], start=False, stop=False), deps=[], tag=f"fcb{mq}f{ft}")
                last = P.add("T", lambda t, ft=ft, bank=bank: t.matmul(
                    bank[:, 0:256], lbfc[:, ft * 128:ft * 128 + 128],
                    rawap(lora_b, 0, 0, [["P", 16], [1, 256]]), start=False, stop=True),
                    deps=[], tag=f"fcl{mq}f{ft}")
                ring_guard_fc[slot] = last
                o_gel.append(P.add("S", lambda sc, ft=ft, bank=bank: sc.activation(
                    out=dslice(h1b, ft * 256, (ft + 1) * 256), in_=bank[:, 0:256],
                    func=AF.Gelu), deps=[last], tag=f"gel{mq}f{ft}"))
            prev_fc_last = o_gel[-1]
            prev_gel[mq] = o_gel[-1]
            if h1_d is not None:
                P.add("Q", lambda s, mq=mq: s.dma_start(
                    out=bass.AP(h1_d.tensor, h1_d.offset + mq * 8192,
                                [[(T // 256) * 8192, 128], [1, 8192]]),
                    in_=h1b.sl(0, 8192)), deps=list(o_gel), dma=True, tag=f"dh1{mq}")
            o_l2mm = None
            for ft in range(32):
                o_l2mm = P.add("T", lambda t, ft=ft: t.matmul(
                    pb[5][0:16, 0:256], lapr[:, ft * 16:(ft + 1) * 16],
                    dslice(h1b, ft * 256, (ft + 1) * 256),
                    start=(ft == 0), stop=(ft == 31)), deps=[o_gel[ft]], tag=f"l2mm{mq}f{ft}")
            o_l2e = P.add("S", lambda sc: sc.mul(
                out=rawap(lora_b, 0, 512, [["P", 16], [1, 256]]), in_=pb[5][0:16, 0:256], mul=2.0),
                deps=[o_l2mm], tag=f"l2ev{mq}")
            lastp = {}
            for ft in range(32):
                slot = ft % 8
                o_ws = P.add("Q", lambda s, ft=ft, slot=slot: s.dma_start(
                    out=dslice(pjW, slot * 1024, (slot + 1) * 1024),
                    in_=bass.AP(wqp_d.tensor, wqp_d.offset + ft * F,
                                [[32 * F, 128], [1, F]])),
                    deps=[o_wq_last["pr"]] + ([ring_guard_pj.get(slot)] if ring_guard_pj.get(slot) else []),
                    dma=True, tag=f"pws{mq}f{ft}")
                for ii in range(2):
                    for nt in range(2):
                        bank = pb[2 + 2 * ii + nt] if (2 + 2 * ii + nt) < 4 else pb[4 + 2 * ii + nt]
                        bank = pb[[2, 3, 6, 7][2 * ii + nt]]
                        dep = []
                        if ft == 0:
                            dep = [o_gel[-1], o_l2e, o_ws] + (o_out_all[-4:] if o_out_all else [])
                        else:
                            dep = [o_ws]
                        lastp[(ii, nt)] = P.add("T", lambda t, ft=ft, ii=ii, nt=nt, bank=bank, slot=slot:
                                                t.matmul(bank[:, :],
                                                         dslice(h1b, ft * 256 + ii * 128, ft * 256 + ii * 128 + 128),
                                                         dslice(pjW, slot * 1024 + nt * 512, slot * 1024 + nt * 512 + 512),
                                                         start=(ft == 0), stop=False),
                                                deps=dep, tag=f"pj{mq}i{ii}n{nt}f{ft}")
                        ring_guard_pj[slot] = lastp[(ii, nt)]
            for ii in range(2):
                i = 2 * mq + ii
                for nt in range(2):
                    bank = pb[[2, 3, 6, 7][2 * ii + nt]]
                    l1 = P.add("T", lambda t, nt=nt, bank=bank: t.matmul(
                        bank[:, :], ones_b[:, 0:128],
                        rawap(rows_b, 0, 1280 + nt * 512, [[4096, 1], [1, 512]]),
                        start=False, stop=False),
                        deps=[lastp[(ii, nt)]], tag=f"pjb{i}n{nt}")
                    l2 = P.add("T", lambda t, nt=nt, bank=bank, ii=ii: t.matmul(
                        bank[:, :], rawap(lora_b, 0, 512 + ii * 128, [["P", 16], [1, 128]]),
                        lbpr[:, nt * 512:nt * 512 + 512], start=False, stop=True),
                        deps=[], tag=f"pjl{i}n{nt}")
                    prev_proj[mq] = l2
                    o_fo = P.add("V", lambda v, ii=ii, nt=nt, bank=bank: v.tensor_tensor(
                        out=rawap(stg, 0, 1024 + nt * 512, [["P", 128], [1, 512]]),
                        in0=x_t[:, (ii % 2) * F + nt * 512:(ii % 2) * F + nt * 512 + 512],
                        in1=bank[:, :], op=ALU.add),
                        deps=[l2] + (o_out_all[-2:] if o_out_all else []), tag=f"fo{i}n{nt}")
                    prev_xuse[(mq, ii)] = o_fo
                    o_out_all.append(P.add("Q", lambda s, i=i, nt=nt: s.dma_start(
                        out=out_d[128 * i:128 * (i + 1), nt * 512:nt * 512 + 512],
                        in_=rawap(stg, 0, 1024 + nt * 512, [["P", 128], [1, 512]])),
                        deps=[o_fo], dma=True, tag=f"outd{i}n{nt}"))
        P.add("Q", lambda s: s.nop(), deps=o_out_all, tag="final_wait")
        P.emit_block()

    return nc



_NC_CACHE = {}


def kernel(**inputs):
    Bt, Tfull, _ = inputs["L0_attn_x"].shape
    n_cores = 8
    spb = n_cores // Bt
    T = Tfull // spb

    hw = host_weights(inputs)
    ht = host_tables()

    key = (T,)
    if key not in _NC_CACHE:
        _NC_CACHE[key] = build_nc(T, n_ranks=n_cores)
    nc = _NC_CACHE[key]

    NC_ = T // C
    in_maps = []
    for core in range(n_cores):
        b = core // spb
        sl = core % spb
        t0 = sl * T
        cosT, sinT = host_rope(Tfull, t0, T)
        msel = np.zeros((1, 8), np.float32)
        gap = np.zeros((1, 8), np.float32)
        for j in range(n_cores):
            jb, jsl = j // spb, j % spb
            if jb == b and jsl < sl:
                msel[0, j] = 1.0
                gap[0, j] = float(T * (sl - 1 - jsl))
        m = {"x": np.ascontiguousarray(inputs["L0_attn_x"][b, t0:t0 + T]),
             "cosT": cosT, "sinT": sinT, "msel": msel, "gap4096": gap,
             "k128row": (128.0 * np.arange(NC_, dtype=np.float32))[None, :]}
        m.update(hw)
        m.update(ht)
        in_maps.append(m)

    res = run_bass_kernel_spmd(nc, in_maps, core_ids=list(range(n_cores)))
    out = np.empty((Bt, Tfull, F), np.float32)
    for core in range(n_cores):
        b = core // spb
        sl = core % spb
        out[b, sl * T:(sl + 1) * T] = res.results[core]["out"]
    return out
